# version 02
# baseline (speedup 1.0000x reference)
import contextlib
import numpy as np
import ml_dtypes

import concourse.bass as bass
import concourse.mybir as mybir
from concourse.bass_utils import run_bass_kernel_spmd
from concourse.alu_op_type import AluOpType

F32 = mybir.dt.float32
BF16 = mybir.dt.bfloat16
AF = mybir.ActivationFunctionType
AX = mybir.AxisListType
OP = AluOpType

D = 1024
H = 16
HD = 64
TC = 4096
NWIN = 16
CC_ENABLE = True
PHASEB = True
STAGE = 8
SAFE5 = False
DEBUG_DUMP = False
LN_EPS = 1e-5
RMS_EPS = 1.1920928955078125e-07
LN448 = float(np.log(np.float32(448.0)))
C20 = float(2.0 ** 20)
C20I = float(2.0 ** -20)
C16 = float(2.0 ** 16)
C16I = float(2.0 ** -16)

PS_Q, PS_K, PS_V = 0, 1024, 2048
PS_T0, PS_T1 = 3072, 3584
PS_TY0, PS_TY1 = 1024, 1568
PS_SM = 3840
PS_SC = 0
PS_S0, PS_S1 = 1536, 1552
PS_Y0, PS_Y1 = 2048, 3072
PS_O = 0
PS_ACC = 0
PS_F = 2048


class _FakeInstr:
    def then_inc(self, *a, **k):
        return self


class _FakeEng:
    def __getattr__(self, name):
        def f(*a, **k):
            return _FakeInstr()
        return f


class Plan:
    def __init__(self):
        self.mode = "count"
        self.counts = {}
        self.events = {}
        self.handles = {}
        self.trace = {}
        self.cur = None

    def reset(self, mode):
        self.mode = mode
        self.counts = {}

    def inc(self, instr, sem, k=1, label=None):
        c = self.counts.get(sem, 0) + k
        self.counts[sem] = c
        if self.mode == "count":
            if label is not None:
                assert label not in self.events, f"dup label {label}"
                self.events[label] = (sem, c)
            self.trace[self.cur].append(("i", sem, k, label))
        else:
            instr.then_inc(self.handles[sem], k)
        return c

    def wait(self, eng, label):
        if self.mode == "count":
            self.trace[self.cur].append(("w", label, None, None))
        elif self.mode == "emit":
            if label not in self.events:
                assert not getattr(self, "strict", True), f"missing {label}"
                return
            sem, val = self.events[label]
            eng.wait_ge(self.handles[sem], val)

    def wait_val(self, eng, sem, val):
        if self.mode == "count":
            self.trace[self.cur].append(("wv", sem, val, None))
        elif self.mode == "emit":
            eng.wait_ge(self.handles[sem], val)

    def check_deadlock(self):
        pcs = {k: 0 for k in self.trace}
        counts = {}
        progressed = True
        while progressed:
            progressed = False
            for eng, items in self.trace.items():
                while pcs[eng] < len(items):
                    kind, a, b, lbl = items[pcs[eng]]
                    if kind == "i":
                        counts[a] = counts.get(a, 0) + b
                    elif kind == "w" and a not in self.events:
                        assert not getattr(self, "strict", True), f"missing {a}"
                        pcs[eng] += 1
                        progressed = True
                        continue
                    else:
                        sem, val = (self.events[a] if kind == "w"
                                    else (a, b))
                        if counts.get(sem, 0) < val:
                            break
                    pcs[eng] += 1
                    progressed = True
        stuck = {e: pcs[e] for e in pcs if pcs[e] < len(self.trace[e])}
        if stuck:
            lines = []
            for eng, pc in stuck.items():
                kind, a, b, lbl = self.trace[eng][pc]
                sem, val = (self.events.get(a, ("?", -1)) if kind == "w"
                            else (a, b))
                lines.append(f"{eng}@{pc}: waits {a if kind == 'w' else sem}"
                             f" (sem {sem} >= {val}, have {counts.get(sem, 0)})")
            raise RuntimeError("plan deadlock:\n" + "\n".join(lines))


def build_program(nwin=NWIN):
    ntile = 2 * nwin
    nc = bass.Bass("TRN2", target_bir_lowering=False,
                   detect_race_conditions=False)

    x_in = nc.dram_tensor("x", [nwin * 256, D], F32, kind="ExternalInput").ap()
    wts = {p: nc.dram_tensor(f"w{p}", [D, D], F32, kind="ExternalInput").ap()
           for p in ("q", "k", "v", "o", "f")}
    bias_in = nc.dram_tensor("bias", [1, D], F32, kind="ExternalInput").ap()
    g_in = nc.dram_tensor("grep", [128, H], F32, kind="ExternalInput").ap()
    cos_in = nc.dram_tensor("cosr", [128, ntile * 32], F32, kind="ExternalInput").ap()
    sin_in = nc.dram_tensor("sinr", [128, ntile * 32], F32, kind="ExternalInput").ap()
    mask_in = nc.dram_tensor("mask", [128, 128], F32, kind="ExternalInput").ap()
    identf_in = nc.dram_tensor("identf", [128, 128], F32, kind="ExternalInput").ap()
    out_hbm = nc.dram_tensor("out", [nwin * 256, D], F32, kind="ExternalOutput").ap()
    dbg = (nc.dram_tensor("dbg", [128, 12288], F32, kind="ExternalOutput").ap()
           if DEBUG_DUMP else None)
    cc_i = nc.dram_tensor("cc_i", [1, 1], F32)
    cc_o = nc.dram_tensor("cc_o", [1, 1], F32)

    P = Plan()
    es = contextlib.ExitStack()
    sb = lambda name, shape, dt: es.enter_context(nc.sbuf_tensor(name, shape, dt))

    wT = {p: sb(f"wT{p}", [128, 8 * D], BF16) for p in ("q", "k", "v", "o")}
    wf8 = sb("wf8", [128, 8 * D], BF16)
    x1 = sb("x1", [128, ntile * D], BF16)
    x_sb = sb("x_sb", [128, 2 * 2048], BF16)
    xn = sb("xn", [128, 2048], BF16)
    xnT = sb("xnT", [128, 2048], BF16)
    qrot = sb("qrot", [128, D], BF16)
    krot = sb("krot", [128, D], BF16)
    qT = sb("qT", [128, 2048], BF16)
    kT = sb("kT", [128, 2048], BF16)
    v_sb = sb("v_sb", [128, 2048], BF16)
    probsT = xnT
    y_sb = qrot
    yT = krot
    sq = sb("sq", [128, D], BF16)
    xmb = sb("xmb", [128, D], BF16)
    cos_sb = sb("cos_sb", [128, ntile * 32], BF16)
    sin_sb = sb("sin_sb", [128, ntile * 32], BF16)
    cq_t = sb("cq_t", [128, 512], BF16)
    sq_t = sb("sq_t", [128, 512], BF16)
    mask_sb = sb("mask_sb", [128, 128], BF16)
    identb = sb("identb", [128, 128], BF16)
    identf = sb("identf_sb", [128, 128], F32)
    ones_col = sb("ones_col", [128, 1], BF16)
    ones_row = sb("ones_row", [1, 128], BF16)
    onesf_row = sb("onesf_row", [1, 128], F32)
    g8 = sb("g8", [128, H], F32)
    bias_sb = sb("bias_sb", [1, D], F32)
    bias_pre = sb("bias_pre", [1, D], BF16)
    t0f = sb("t0f", [128, D], F32)
    zf = sb("zf", [128, D], F32)
    outb = sb("outb", [128, D], F32)
    st_bn = sb("st_bn", [128, 24], F32)
    st_mv = sb("st_mv", [128, 8], F32)
    st_r = sb("st_r", [128, 4], F32)
    st_m = sb("st_m", [128, 64], F32)
    st_R = sb("st_R", [128, 64], F32)
    st_s = sb("st_s", [128, 32], F32)
    st_r3 = sb("st_r3", [128, ntile], F32)
    st_nm3 = sb("st_nm3", [128, ntile], F32)
    st_rsx = sb("st_rsx", [128, ntile], F32)
    st_nmx = sb("st_nmx", [128, ntile], F32)
    st_run = sb("st_run", [128, 1], F32)
    st_mm = sb("st_mm", [128, 2], F32)
    st_sc = sb("st_sc", [1, 8], F32)
    st_bc = sb("st_bc", [128, 2], F32)
    st_e = sb("st_e", [128, 4], F32)
    st_snap = sb("st_snap", [128, 28], F32) if DEBUG_DUMP else None
    st_eps = sb("st_eps", [128, 2], F32)
    st_nop = sb("st_nop", [128, 1], F32)
    psum = es.enter_context(nc.psum_tensor("ps", [128, 4096], F32))

    sems = {}
    for name in ("pe", "v", "s", "g", "dg", "dx0", "dx1", "dwt", "dwf0",
                 "dwf1", "dout", "cc"):
        sems[name] = es.enter_context(nc.semaphore(name))
    P.handles = sems

    def heads3(ap):
        return ap.rearrange("p (h j) -> p h j", h=H)

    def half3(ap, lo):
        return ap.rearrange("p (h j) -> p h j", h=H)[:, :, lo:lo + 32]

    def bcast_h(ap32):
        return ap32.unsqueeze(1).broadcast_to([128, H, 32])

    def prog_sync(e):
        if PHASEB:
            for T in range(ntile):
                P.wait(e, f"outb_{T}")
                i = e.dma_start(out=out_hbm[T * 128:(T + 1) * 128, :],
                                in_=outb[:, :])
                P.inc(i, "dout", 16, label=f"out_{T}")
            dbg_extra = (160 if PHASEB else 80) if DEBUG_DUMP else 0
            P.wait_val(e, "dout", 16 * ntile + dbg_extra)

    def prog_gpsimd(e):
        for p in ("q", "k", "v", "o"):
            i = e.dma_start(
                out=wT[p][:, :].rearrange("p (c j) -> p c j", c=8),
                in_=wts[p].rearrange("(c p) j -> p c j", c=8))
            P.inc(i, "dwt", 16)
        for dst, srcap in ((cos_sb, cos_in), (sin_sb, sin_in),
                           (mask_sb, mask_in), (identb, identf_in),
                           (identf, identf_in), (g8, g_in),
                           (bias_sb, bias_in)):
            i = e.dma_start(out=dst[:, :], in_=srcap)
            P.inc(i, "dwt", 16,
                  label=("setup_done" if dst is bias_sb else None))
        def load_x(w):
            if w >= 2:
                P.wait(e, f"xfree_{w - 2}")
            for tau in range(2):
                i = e.dma_start(
                    out=x_sb[:, (w % 2) * 2048 + tau * 1024:
                             (w % 2) * 2048 + (tau + 1) * 1024],
                    in_=x_in[w * 256 + tau * 128: w * 256 + tau * 128 + 128, :])
                P.inc(i, "dx0" if w % 2 == 0 else "dx1", 16,
                      label=(f"x_{w}" if tau == 1 else None))
        load_x(0)
        if nwin > 1:
            load_x(1)
        for c in (range(8) if PHASEB else []):
            buf = t0f if c % 2 == 0 else zf
            if c >= 2:
                P.wait(e, f"wfa_{c - 2}")
            i = e.dma_start(out=buf[:, :], in_=wts["f"][c * 128:(c + 1) * 128, :])
            P.inc(i, "dwf0" if c % 2 == 0 else "dwf1", 16, label=f"wf1_{c}")
            P.wait(e, f"wfr_{c}")
            i = e.engine_nop()
            P.inc(i, "g", label=f"wfs_{c}")
        for c in (range(8) if PHASEB else []):
            buf = t0f if c % 2 == 0 else zf
            P.wait(e, "wfa_7" if c < 2 else f"wfq_sub_{c - 2}")
            i = e.dma_start(out=buf[:, :], in_=wts["f"][c * 128:(c + 1) * 128, :])
            P.inc(i, "dwf0" if c % 2 == 0 else "dwf1", 16, label=f"wf2_{c}")
        for w in range(nwin):
            if STAGE >= 3:
                P.wait(e, f"vev_{w}_1")
            for tau in (range(2) if STAGE >= 7 else []):
                P.wait(e, f"a1_{w}_{tau}")
                i = e.engine_nop()
                P.inc(i, "g", label=f"a2_{w}_{tau}")
            if w + 2 < nwin:
                load_x(w + 2)
        if DEBUG_DUMP:
            P.wait(e, f"x1_{nwin - 1}_1")
            for name, t, width in (("qT", qT, 2048), ("kT", kT, 2048),
                                   ("pT", xnT, 2048), ("y", y_sb, 1024),
                                   ("v", v_sb, 2048)):
                pass
            offs = 0
            for t, width in ((xn, 2048), (v_sb, 2048), (qT, 2048),
                             (kT, 2048), (x1, 2048)):
                i = e.dma_start(out=dbg[:, offs:offs + width], in_=t[:, 0:width])
                P.inc(i, "dout", 16)
                offs += width
        if not PHASEB:
            if STAGE >= 7:
                for T in range(ntile):
                    P.wait(e, f"x1_{T // 2}_{T % 2}")
                    i = e.dma_start(out=out_hbm[T * 128:(T + 1) * 128, :],
                                    in_=x1[:, T * D:(T + 1) * D])
                    P.inc(i, "dout", 16)
            else:
                P.wait(e, f"xfree_{nwin - 1}")
                for T in range(ntile):
                    i = e.dma_start(out=out_hbm[T * 128:(T + 1) * 128, :],
                                    in_=x_sb[:, (T % 4) * 1024:
                                             (T % 4) * 1024 + 1024])
                    P.inc(i, "dout", 16)
            P.wait_val(e, "dout", 16 * ntile)
            return
        P.wait(e, "amaxc")
        i = e.dma_start(out=cc_i.ap(), in_=st_sc[0:1, 3:4])
        P.inc(i, "dg", 16)
        P.wait_val(e, "dg", P.counts["dg"])
        if CC_ENABLE:
            i = e.collective_compute(
                "AllReduce", OP.max, replica_groups=[list(range(8))],
                ins=[cc_i.ap().opt()], outs=[cc_o.ap().opt()])
            P.inc(i, "cc", 1)
            P.wait_val(e, "cc", 1)
        else:
            i = e.dma_start(out=cc_o.ap(), in_=cc_i.ap())
            P.inc(i, "cc", 16)
            P.wait_val(e, "cc", 16)
        i = e.dma_start(out=st_sc[0:1, 4:5], in_=cc_o.ap())
        P.inc(i, "dg", 16, label="amax_g")
        for T in range(ntile):
            P.wait(e, f"ffn_t0_{T}")
            i = e.tensor_scalar(t0f[:, :], t0f[:, :], 448.0, -448.0,
                                OP.min, OP.max)
            P.inc(i, "g", label=f"ffn_tc_{T}")
            if T > 0:
                P.wait(e, f"Tq8_{T - 1}_7")
            P.wait(e, f"ffn_u_{T}")
            i = e.tensor_tensor(sq[:, :], zf[:, :], t0f[:, :], OP.subtract)
            P.inc(i, "g", label=f"ffn_q8_{T}")
        if DEBUG_DUMP:
            P.wait(e, f"outb_{ntile - 1}")
            i = e.dma_start(out=dbg[0:1, 12000:12008], in_=st_sc[0:1, 0:8])
            P.inc(i, "dout", 16)
            offs2 = 10240
            for t, width in ((st_r3, ntile), (st_nm3, ntile),
                             (st_run, 1), (st_mm, 2)):
                i = e.dma_start(out=dbg[:, offs2:offs2 + width],
                                in_=t[:, 0:width],
                                allow_slow_non_contiguous=True)
                P.inc(i, "dout", 16)
                offs2 += width

    def prog_scalar(e):
        if PHASEB:
            P.wait(e, "wam_d")
            i = e.activation(st_e[0:1, 3:4], st_sc[0:1, 0:1], AF.Ln)
            P.inc(i, "s", label="wln_d")
            e.copy(st_nop[:, 0:1], st_nop[:, 0:1])
            e.copy(st_nop[:, 0:1], st_nop[:, 0:1])
            P.wait(e, "wl2")
            i = e.activation(st_sc[0:1, 1:2], st_e[0:1, 3:4], AF.Exp)
            P.inc(i, "s", label="wsc_exp")
            P.wait(e, "scale_w")
            i = e.activation(st_e[0:1, 3:4], st_sc[0:1, 1:2], AF.Ln)
            P.inc(i, "s")
            e.copy(st_nop[:, 0:1], st_nop[:, 0:1])
            e.copy(st_nop[:, 0:1], st_nop[:, 0:1])
            i = e.activation(st_sc[0:1, 2:3], st_e[0:1, 3:4], AF.Exp,
                             scale=-1.0)
            P.inc(i, "s", label="wsr")
            for c in range(8):
                P.wait(e, f"wfq_tc_{c}")
                if c > 0:
                    P.wait(e, f"wfq_u_{c - 1}")
                buf = t0f if c % 2 == 0 else zf
                i = e.activation(outb[:, :], buf[:, :], AF.Identity, scale=C20)
                P.inc(i, "s", label=f"wfq_z_{c}")
        for w in range(nwin):
            sl = (w % 2) * 2048
            for tau in range(2):
                P.wait(e, f"bnx_eps_{w}_{tau}")
                i = e.activation(st_e[:, 0:1],
                                 st_mv[:, tau * 2 + 1:tau * 2 + 2], AF.Ln,
                                 bias=st_eps[:, 0:1])
                P.inc(i, "s")
                e.copy(st_nop[:, 0:1], st_nop[:, 0:1])
                e.copy(st_nop[:, 0:1], st_nop[:, 0:1])
                i = e.activation(st_r[:, tau * 2:tau * 2 + 1], st_e[:, 0:1],
                                 AF.Exp, scale=-0.5)
                P.inc(i, "s", label=f"r1_{w}_{tau}")
                P.wait(e, f"nmr_{w}_{tau}")
                if w > 0 and tau == 0:
                    P.wait(e, f"TxnT_{w - 1}_7")
                i = e.activation(xn[:, tau * 1024:(tau + 1) * 1024],
                                 x_sb[:, sl + tau * 1024: sl + (tau + 1) * 1024],
                                 AF.Identity,
                                 scale=st_r[:, tau * 2:tau * 2 + 1],
                                 bias=st_r[:, tau * 2 + 1:tau * 2 + 2])
                P.inc(i, "s", label=f"xn_{w}_{tau}")
            for c in range(8):
                P.wait(e, f"TxnT_{w}_{c}")
                base = PS_T0 if c % 2 == 0 else PS_T1
                i = e.copy(xnT[:, c * 256:(c + 1) * 256],
                           psum[:, base:base + 256])
                P.inc(i, "s", label=f"xnTe_{w}_{c}")
            for tau in (range(2) if STAGE >= 3 else []):
                P.wait(e, f"qkv_{w}_{tau}")
                if not (w == 0 and tau == 0):
                    pw, pt = (w, 0) if tau == 1 else (w - 1, 1)
                    P.wait(e, f"mk_{pw}_{pt}")
                i = e.activation(sq[:, :], psum[:, PS_Q:PS_Q + 1024], AF.Square)
                P.inc(i, "s", label=f"sqq_{w}_{tau}")
                P.wait(e, f"mq_{w}_{tau}")
                i = e.activation(sq[:, :], psum[:, PS_K:PS_K + 1024], AF.Square)
                P.inc(i, "s", label=f"sqk_{w}_{tau}")
                if w > 0 and tau == 0:
                    P.wait(e, f"pv_{w - 1}")
                i = e.copy(v_sb[:, tau * 1024:(tau + 1) * 1024],
                           psum[:, PS_V:PS_V + 1024])
                P.inc(i, "s", label=f"vev_{w}_{tau}")
                P.wait(e, f"mk_{w}_{tau}")
                i = e.activation(st_R[:, tau * 32:tau * 32 + 16],
                                 st_m[:, tau * 32:tau * 32 + 16], AF.Ln,
                                 scale=1.0 / 64.0, bias=st_eps[:, 1:2])
                P.inc(i, "s")
                e.copy(st_nop[:, 0:1], st_nop[:, 0:1])
                e.copy(st_nop[:, 0:1], st_nop[:, 0:1])
                i = e.activation(st_R[:, tau * 32:tau * 32 + 16],
                                 st_R[:, tau * 32:tau * 32 + 16], AF.Exp,
                                 scale=-0.5)
                P.inc(i, "s", label=f"Rq_{w}_{tau}")
                i = e.activation(st_R[:, tau * 32 + 16:tau * 32 + 32],
                                 st_m[:, tau * 32 + 16:tau * 32 + 32], AF.Ln,
                                 scale=1.0 / 64.0, bias=st_eps[:, 1:2])
                P.inc(i, "s")
                e.copy(st_nop[:, 0:1], st_nop[:, 0:1])
                e.copy(st_nop[:, 0:1], st_nop[:, 0:1])
                i = e.activation(st_R[:, tau * 32 + 16:tau * 32 + 32],
                                 st_R[:, tau * 32 + 16:tau * 32 + 32], AF.Exp,
                                 scale=-0.5)
                P.inc(i, "s", label=f"Rk_{w}_{tau}")
            for tau in (range(2) if STAGE >= 4 else []):
                for c in range(8):
                    P.wait(e, f"TqT_{w}_{tau}_{c}")
                    base = PS_T0 if c % 2 == 0 else PS_T1
                    i = e.copy(qT[:, c * 256 + tau * 128:
                                  c * 256 + tau * 128 + 128],
                               psum[:, base:base + 128])
                    P.inc(i, "s", label=f"qTe_{w}_{tau}_{c}")
                    P.wait(e, f"TkT_{w}_{tau}_{c}")
                    i = e.copy(kT[:, c * 256 + tau * 128:
                                  c * 256 + tau * 128 + 128],
                               psum[:, base + 128:base + 256])
                    P.inc(i, "s", label=f"kTe_{w}_{tau}_{c}")
            for gi in (range(4) if STAGE >= 5.5 else []):
                P.wait(e, f"sc_{w}_{gi}")
                for hi in range(4):
                    h = 4 * gi + hi
                    i = e.activation(probsT[:, 256 * hi:256 * hi + 256],
                                     psum[:, PS_SC + 256 * hi:
                                          PS_SC + 256 * hi + 256],
                                     AF.Exp, scale=st_R[:, 16 + h:16 + h + 1])
                    P.inc(i, "s")
                    i = e.activation(probsT[:, 1024 + 128 * hi:
                                            1024 + 128 * hi + 128],
                                     psum[:, 1024 + 128 * hi:
                                          1024 + 128 * hi + 128],
                                     AF.Exp, scale=st_R[:, 48 + h:48 + h + 1])
                    P.inc(i, "s", label=(f"exp_{w}_{gi}" if hi == 3 else None))
            for tau in (range(2) if STAGE >= 6 else []):
                P.wait(e, f"pv_{w}")
                base = PS_S0 if tau == 0 else PS_S1
                i = e.activation(st_s[:, tau * 16:tau * 16 + 16],
                                 psum[:, base:base + 16], AF.Ln)
                P.inc(i, "s")
                e.copy(st_nop[:, 0:1], st_nop[:, 0:1])
                e.copy(st_nop[:, 0:1], st_nop[:, 0:1])
                i = e.activation(st_s[:, tau * 16:tau * 16 + 16],
                                 st_s[:, tau * 16:tau * 16 + 16], AF.Exp,
                                 scale=-1.0)
                P.inc(i, "s", label=f"RS_{w}_{tau}")
            for tau in (range(2) if STAGE >= 7 else []):
                for c in range(8):
                    P.wait(e, f"TyT_{w}_{tau}_{c}")
                    base = PS_TY0 if c % 2 == 0 else PS_TY1
                    i = e.copy(yT[:, c * 128:(c + 1) * 128],
                               psum[:, base:base + 128])
                    P.inc(i, "s", label=f"yTe_{w}_{tau}_{c}")
                P.wait(e, f"bno_eps_{w}_{tau}")
                i = e.activation(st_e[:, 1:2],
                                 st_mv[:, 4 + tau * 2 + 1:4 + tau * 2 + 2],
                                 AF.Ln, bias=st_eps[:, 0:1])
                P.inc(i, "s")
                e.copy(st_nop[:, 0:1], st_nop[:, 0:1])
                e.copy(st_nop[:, 0:1], st_nop[:, 0:1])
                i = e.activation(st_r[:, tau * 2:tau * 2 + 1], st_e[:, 1:2],
                                 AF.Exp, scale=-0.5)
                P.inc(i, "s", label=f"r2_{w}_{tau}")
                P.wait(e, f"nm2_{w}_{tau}")
                i = e.activation(xmb[:, :],
                                 x_sb[:, (w % 2) * 2048 + tau * 1024:
                                      (w % 2) * 2048 + (tau + 1) * 1024],
                                 AF.Identity,
                                 bias=st_r[:, tau * 2 + 1:tau * 2 + 2])
                P.inc(i, "s", label=f"xmb_{w}_{tau}")
                T = 2 * w + tau
                P.wait(e, f"bn3_eps_{w}_{tau}")
                i = e.activation(st_e[:, 2:3],
                                 st_mv[:, 4 + tau * 2 + 1:4 + tau * 2 + 2],
                                 AF.Ln, bias=st_eps[:, 0:1])
                P.inc(i, "s")
                e.copy(st_nop[:, 0:1], st_nop[:, 0:1])
                e.copy(st_nop[:, 0:1], st_nop[:, 0:1])
                i = e.activation(st_r3[:, T:T + 1], st_e[:, 2:3], AF.Exp,
                                 scale=-0.5)
                P.inc(i, "s", label=f"r3_{w}_{tau}")
        if not PHASEB:
            return
        P.wait(e, "amax_d")
        i = e.activation(st_e[0:1, 3:4], st_sc[0:1, 4:5], AF.Ln)
        P.inc(i, "s", label="sc_ln1")
        e.copy(st_nop[:, 0:1], st_nop[:, 0:1])
        e.copy(st_nop[:, 0:1], st_nop[:, 0:1])
        P.wait(e, "sc_l2")
        i = e.activation(st_sc[0:1, 5:6], st_e[0:1, 3:4], AF.Exp)
        P.inc(i, "s", label="sc_exp1")
        P.wait(e, "sc_sx")
        i = e.activation(st_e[0:1, 3:4], st_sc[0:1, 5:6], AF.Ln)
        P.inc(i, "s")
        e.copy(st_nop[:, 0:1], st_nop[:, 0:1])
        e.copy(st_nop[:, 0:1], st_nop[:, 0:1])
        i = e.activation(st_sc[0:1, 6:7], st_e[0:1, 3:4], AF.Exp, scale=-1.0)
        P.inc(i, "s", label="sc_xs")
        P.wait(e, "sc_sws")
        i = e.activation(st_e[0:1, 3:4], st_sc[0:1, 7:8], AF.Ln)
        P.inc(i, "s")
        e.copy(st_nop[:, 0:1], st_nop[:, 0:1])
        e.copy(st_nop[:, 0:1], st_nop[:, 0:1])
        i = e.activation(st_sc[0:1, 2:3], st_e[0:1, 3:4], AF.Exp, scale=-1.0)
        P.inc(i, "s", label="sc_swsr")
        for T in range(ntile):
            P.wait(e, f"ffn_tc_{T}")
            if T > 0:
                P.wait(e, f"outb_{T - 1}")
            i = e.activation(zf[:, :], t0f[:, :], AF.Identity, scale=C20)
            P.inc(i, "s", label=f"ffn_z_{T}")
            for c in range(8):
                P.wait(e, f"Tq8_{T}_{c}")
                base = PS_T0 if c % 2 == 0 else PS_T1
                i = e.copy(xmb[:, c * 128:(c + 1) * 128],
                           psum[:, base:base + 128])
                P.inc(i, "s", label=f"q8Te_{T}_{c}")
            P.wait(e, f"bn4_eps_{T}")
            i = e.activation(st_e[:, 2:3],
                             st_mv[:, 1:2], AF.Ln, bias=st_eps[:, 0:1])
            P.inc(i, "s")
            e.copy(st_nop[:, 0:1], st_nop[:, 0:1])
            e.copy(st_nop[:, 0:1], st_nop[:, 0:1])
            i = e.activation(st_r[:, 0:1], st_e[:, 2:3], AF.Exp, scale=-0.5)
            P.inc(i, "s", label=f"r4_{T}")
            P.wait(e, f"nm4_{T}")
            i = e.activation(zf[:, :], psum[:, PS_F:PS_F + 1024], AF.Identity,
                             scale=st_r[:, 0:1], bias=st_r[:, 1:2])
            P.inc(i, "s", label=f"o1_{T}")

    def prog_vector(e):
        i = e.memset(st_nop[:, 0:1], 0.0)
        P.inc(i, "v")
        i = e.memset(st_eps[:, 0:1], LN_EPS)
        P.inc(i, "v")
        i = e.memset(st_eps[:, 1:2], RMS_EPS)
        P.inc(i, "v")
        i = e.memset(ones_col[:, :], 1.0)
        P.inc(i, "v")
        i = e.memset(ones_row[:, :], 1.0)
        P.inc(i, "v")
        i = e.memset(onesf_row[:, :], 1.0)
        P.inc(i, "v")
        i = e.memset(st_run[:, :], 0.0)
        P.inc(i, "v")
        i = e.memset(st_mm[:, 0:1], 0.0)
        P.inc(i, "v")
        P.wait(e, "setup_done")
        i = e.tensor_scalar(g8[:, :], g8[:, :], 0.125, None, OP.mult)
        P.inc(i, "v", label="g8")
        for c in range(8):
            buf = t0f if c % 2 == 0 else zf
            P.wait(e, f"wf1_{c}")
            i = e.tensor_reduce(st_mm[:, 1:2], buf[:, :], AX.X, OP.max)
            P.inc(i, "v")
            i = e.tensor_reduce(st_e[:, 3:4], buf[:, :], AX.X, OP.min)
            P.inc(i, "v", label=f"wfr_{c}")
            P.wait(e, f"wfs_{c}")
            i = e.tensor_scalar(st_e[:, 3:4], st_e[:, 3:4], -1.0, None,
                                OP.mult)
            P.inc(i, "v")
            for _ in range(8):
                i = e.engine_nop()
                P.inc(i, "v")
            i = e.tensor_tensor(st_mm[:, 1:2], st_mm[:, 1:2], st_e[:, 3:4],
                                OP.max)
            P.inc(i, "v")
            for _ in range(8):
                i = e.engine_nop()
                P.inc(i, "v")
            i = e.tensor_tensor(st_mm[:, 0:1], st_mm[:, 0:1], st_mm[:, 1:2],
                                OP.max)
            P.inc(i, "v", label=f"wfa_{c}")
        P.wait(e, "Twam")
        i = e.reduce_max(st_sc[0:1, 0:1], psum[0:1, PS_SM:PS_SM + 128], AX.X)
        P.inc(i, "v")
        for _ in range(4):
            i = e.engine_nop()
            P.inc(i, "v")
        i = e.tensor_scalar(st_sc[0:1, 0:1], st_sc[0:1, 0:1], 1e-12, None,
                            OP.max)
        P.inc(i, "v", label="wam_d")
        P.wait(e, "wln_d")
        i = e.tensor_scalar(st_e[0:1, 3:4], st_e[0:1, 3:4], -1.0, LN448,
                            OP.mult, OP.add)
        P.inc(i, "v", label="wl2")
        P.wait(e, "wsc_exp")
        i = e.tensor_scalar(st_sc[0:1, 1:2], st_sc[0:1, 1:2], 448.0, None,
                            OP.min)
        P.inc(i, "v", label="scale_w")
        P.wait(e, "Twbc")
        i = e.tensor_copy(st_bc[:, 0:1], psum[:, PS_SM + 128:PS_SM + 129])
        P.inc(i, "v", label="wbc")
        for c in range(8):
            buf = t0f if c % 2 == 0 else zf
            P.wait(e, f"wf2_{c}")
            i = e.tensor_scalar(buf[:, :], buf[:, :], st_bc[:, 0:1], 448.0,
                                OP.mult, OP.min)
            P.inc(i, "v")
            i = e.tensor_scalar(buf[:, :], buf[:, :], -448.0, None, OP.max)
            P.inc(i, "v", label=f"wfq_tc_{c}")
            P.wait(e, f"wfq_z_{c}")
            i = e.scalar_tensor_tensor(buf[:, :], outb[:, :], C20I, outb[:, :],
                                       OP.mult, OP.add)
            P.inc(i, "v", label=f"wfq_u_{c}")
            i = e.tensor_tensor(
                wf8[:, :].rearrange("p (c j) -> p c j", c=8)[:, c, :],
                outb[:, :], buf[:, :], OP.subtract)
            P.inc(i, "v", label=f"wfq_sub_{c}")
        for w in range(nwin):
            sl = (w % 2) * 2048
            for tau in range(2):
                P.wait(e, f"x_{w}")
                xt = x_sb[:, sl + tau * 1024: sl + (tau + 1) * 1024]
                i = e.bn_stats(st_bn[:, tau * 12:tau * 12 + 6], xt[:, 0:512])
                P.inc(i, "v")
                i = e.bn_stats(st_bn[:, tau * 12 + 6:tau * 12 + 12],
                               xt[:, 512:1024])
                P.inc(i, "v")
                i = e.engine_nop()
                P.inc(i, "v")
                i = e.engine_nop()
                P.inc(i, "v")
                i = e.engine_nop()
                P.inc(i, "v")
                i = e.engine_nop()
                P.inc(i, "v")
                i = e.bn_aggr(st_mv[:, tau * 2:tau * 2 + 2].rearrange("p (a b) -> p a b", a=1),
                          st_bn[:, tau * 12:tau * 12 + 12].rearrange("p (g s) -> p g s", s=6))
                P.inc(i, "v", label=f"bnx_eps_{w}_{tau}")
                if DEBUG_DUMP and w == 0:
                    i = e.engine_nop()
                    P.inc(i, "v")
                    i = e.engine_nop()
                    P.inc(i, "v")
                    i = e.tensor_scalar(st_snap[:, tau * 14:tau * 14 + 12],
                                        st_bn[:, tau * 12:tau * 12 + 12],
                                        0.0, None, OP.add)
                    P.inc(i, "v")
                    i = e.tensor_scalar(st_snap[:, tau * 14 + 12:tau * 14 + 14],
                                        st_mv[:, tau * 2:tau * 2 + 2],
                                        0.0, None, OP.add)
                    P.inc(i, "v")
                P.wait(e, f"r1_{w}_{tau}")
                i = e.tensor_tensor(st_r[:, tau * 2 + 1:tau * 2 + 2],
                                    st_mv[:, tau * 2:tau * 2 + 1],
                                    st_r[:, tau * 2:tau * 2 + 1], OP.mult)
                P.inc(i, "v")
                i = e.engine_nop()
                P.inc(i, "v")
                i = e.engine_nop()
                P.inc(i, "v")
                i = e.tensor_scalar(st_r[:, tau * 2 + 1:tau * 2 + 2],
                                    st_r[:, tau * 2 + 1:tau * 2 + 2],
                                    -1.0, None, OP.mult)
                P.inc(i, "v", label=f"nmr_{w}_{tau}")
            for tau in (range(2) if STAGE >= 3 else []):
                P.wait(e, f"sqq_{w}_{tau}")
                i = e.tensor_reduce(st_m[:, tau * 32:tau * 32 + 16],
                                    heads3(sq[:, :]), AX.X, OP.add)
                P.inc(i, "v", label=f"mq_{w}_{tau}")
                P.wait(e, f"sqk_{w}_{tau}")
                i = e.tensor_reduce(st_m[:, tau * 32 + 16:tau * 32 + 32],
                                    heads3(sq[:, :]), AX.X, OP.add)
                P.inc(i, "v", label=f"mk_{w}_{tau}")
                P.wait(e, f"Rq_{w}_{tau}")
                i = e.tensor_tensor(st_R[:, tau * 32:tau * 32 + 16],
                                    st_R[:, tau * 32:tau * 32 + 16],
                                    g8[:, :], OP.mult)
                P.inc(i, "v", label=f"Cq_{w}_{tau}")
                i = e.engine_nop()
                P.inc(i, "v")
                i = e.engine_nop()
                P.inc(i, "v")
                Tg = 2 * w + tau
                ct = cos_sb[:, 32 * Tg:32 * Tg + 32]
                st = sin_sb[:, 32 * Tg:32 * Tg + 32]
                Cq3 = st_R[:, tau * 32:tau * 32 + 16].unsqueeze(2)
                i = e.tensor_tensor(
                    cq_t[:, :].rearrange("p (h j) -> p h j", h=H),
                    bcast_h(ct), Cq3.broadcast_to([128, H, 32]), OP.mult)
                P.inc(i, "v")
                i = e.tensor_tensor(
                    sq_t[:, :].rearrange("p (h j) -> p h j", h=H),
                    bcast_h(st), Cq3.broadcast_to([128, H, 32]), OP.mult)
                P.inc(i, "v", label=f"cqt_{w}_{tau}")
                if tau == 1:
                    P.wait(e, f"TqT_{w}_0_7")
                    P.wait(e, f"TkT_{w}_0_7")
                elif w > 0:
                    P.wait(e, f"TqT_{w - 1}_1_7")
                    P.wait(e, f"TkT_{w - 1}_1_7")
                    P.wait(e, f"TyT_{w - 1}_1_7")
                    P.wait(e, f"oproj_{w - 1}_1")
                q_ps = psum[:, PS_Q:PS_Q + 1024]
                k_ps = psum[:, PS_K:PS_K + 1024]
                ct3, st3 = bcast_h(ct), bcast_h(st)
                cq3 = cq_t[:, :].rearrange("p (h j) -> p h j", h=H)
                sq3 = sq_t[:, :].rearrange("p (h j) -> p h j", h=H)
                tA = t0f[:, 0:512].rearrange("p (h j) -> p h j", h=H)
                tB = zf[:, 0:512].rearrange("p (h j) -> p h j", h=H)
                for (dst, src, c3, s3) in ((qrot, q_ps, cq3, sq3),
                                           (krot, k_ps, ct3, st3)):
                    i = e.tensor_tensor(tA, half3(src, 0), c3, OP.mult)
                    P.inc(i, "v")
                    i = e.tensor_tensor(tB, half3(src, 32), s3, OP.mult)
                    P.inc(i, "v")
                    i = e.tensor_tensor(half3(dst[:, :], 0), tA, tB, OP.add)
                    P.inc(i, "v")
                    i = e.tensor_tensor(tA, half3(src, 32), c3, OP.mult)
                    P.inc(i, "v")
                    i = e.tensor_tensor(tB, half3(src, 0), s3, OP.mult)
                    P.inc(i, "v")
                    i = e.tensor_tensor(half3(dst[:, :], 32), tA, tB,
                                        OP.subtract)
                    P.inc(i, "v",
                          label=("ropeq" if dst is qrot else "ropek")
                          + f"_{w}_{tau}")
            for gi in (range(4) if STAGE >= 5.7 else []):
                P.wait(e, f"exp_{w}_{gi}")
                m3 = mask_sb[:, :].unsqueeze(1).broadcast_to([128, 4, 128])
                p6 = probsT[:, 0:1536].rearrange("p (i q) -> p i q", i=6)
                i = e.tensor_tensor(p6[:, 0:4, 0:128], p6[:, 0:4, 0:128], m3,
                                    OP.mult)
                P.inc(i, "v")
                p4 = probsT[:, 1024:1536].rearrange("p (i q) -> p i q", i=4)
                i = e.tensor_tensor(p4, p4, m3, OP.mult)
                P.inc(i, "v", label=f"mask_{w}_{gi}")
            for tau in (range(2) if STAGE >= 6 else []):
                P.wait(e, f"RS_{w}_{tau}")
                if tau == 1:
                    P.wait(e, f"TyT_{w}_0_7")
                elif w > 0:
                    P.wait(e, f"TyT_{w - 1}_1_7")
                ybase = PS_Y0 if tau == 0 else PS_Y1
                rs3 = st_s[:, tau * 16:tau * 16 + 16].unsqueeze(2)
                i = e.tensor_tensor(heads3(y_sb[:, :]),
                                    heads3(psum[:, ybase:ybase + 1024]),
                                    rs3.broadcast_to([128, H, HD]), OP.mult)
                P.inc(i, "v", label=f"ynorm_{w}_{tau}")
            for tau in (range(2) if STAGE >= 7 else []):
                P.wait(e, f"oproj_{w}_{tau}")
                ob = PS_O + tau * 1024
                i = e.bn_stats(st_bn[:, tau * 12:tau * 12 + 6],
                               psum[:, ob:ob + 512])
                P.inc(i, "v")
                i = e.bn_stats(st_bn[:, tau * 12 + 6:tau * 12 + 12],
                               psum[:, ob + 512:ob + 1024])
                P.inc(i, "v")
                i = e.engine_nop()
                P.inc(i, "v")
                i = e.engine_nop()
                P.inc(i, "v")
                i = e.engine_nop()
                P.inc(i, "v")
                i = e.engine_nop()
                P.inc(i, "v")
                i = e.bn_aggr(st_mv[:, 4 + tau * 2:4 + tau * 2 + 2].rearrange("p (a b) -> p a b", a=1),
                          st_bn[:, tau * 12:tau * 12 + 12].rearrange("p (g s) -> p g s", s=6))
                P.inc(i, "v", label=f"bno_eps_{w}_{tau}")
                P.wait(e, f"r2_{w}_{tau}")
                i = e.tensor_tensor(st_r[:, tau * 2 + 1:tau * 2 + 2],
                                    st_mv[:, 4 + tau * 2:4 + tau * 2 + 1],
                                    st_r[:, tau * 2:tau * 2 + 1], OP.mult)
                P.inc(i, "v")
                i = e.engine_nop()
                P.inc(i, "v")
                i = e.engine_nop()
                P.inc(i, "v")
                i = e.tensor_scalar(st_r[:, tau * 2 + 1:tau * 2 + 2],
                                    st_r[:, tau * 2 + 1:tau * 2 + 2], -1.0,
                                    None, OP.mult)
                P.inc(i, "v", label=f"nm2_{w}_{tau}")
                T = 2 * w + tau
                P.wait(e, f"xmb_{w}_{tau}")
                i = e.scalar_tensor_tensor(x1[:, T * D:(T + 1) * D],
                                           psum[:, ob:ob + 1024],
                                           st_r[:, tau * 2:tau * 2 + 1],
                                           xmb[:, :], OP.mult, OP.add)
                P.inc(i, "v", label=f"x1_{w}_{tau}")
                i = e.bn_stats(st_bn[:, tau * 12:tau * 12 + 6],
                               x1[:, T * D:T * D + 512])
                P.inc(i, "v")
                i = e.bn_stats(st_bn[:, tau * 12 + 6:tau * 12 + 12],
                               x1[:, T * D + 512:T * D + 1024])
                P.inc(i, "v")
                i = e.engine_nop()
                P.inc(i, "v")
                i = e.engine_nop()
                P.inc(i, "v")
                i = e.engine_nop()
                P.inc(i, "v")
                i = e.engine_nop()
                P.inc(i, "v")
                i = e.bn_aggr(st_mv[:, 4 + tau * 2:4 + tau * 2 + 2].rearrange("p (a b) -> p a b", a=1),
                          st_bn[:, tau * 12:tau * 12 + 12].rearrange("p (g s) -> p g s", s=6))
                P.inc(i, "v", label=f"bn3_eps_{w}_{tau}")
                P.wait(e, f"r3_{w}_{tau}")
                i = e.tensor_tensor(st_nm3[:, T:T + 1],
                                    st_mv[:, 4 + tau * 2:4 + tau * 2 + 1],
                                    st_r3[:, T:T + 1], OP.mult)
                P.inc(i, "v")
                i = e.engine_nop()
                P.inc(i, "v")
                i = e.engine_nop()
                P.inc(i, "v")
                i = e.tensor_scalar(st_nm3[:, T:T + 1], st_nm3[:, T:T + 1],
                                    -1.0, None, OP.mult)
                P.inc(i, "v", label=f"nm3_{w}_{tau}")
                i = e.tensor_reduce(st_mm[:, 0:1], x1[:, T * D:(T + 1) * D],
                                    AX.X, OP.max)
                P.inc(i, "v")
                i = e.tensor_reduce(st_mm[:, 1:2], x1[:, T * D:(T + 1) * D],
                                    AX.X, OP.min)
                P.inc(i, "v", label=f"a1_{w}_{tau}")
                P.wait(e, f"a2_{w}_{tau}")
                i = e.tensor_scalar(st_mm[:, 0:1], st_mm[:, 0:1],
                                    st_r3[:, T:T + 1], st_nm3[:, T:T + 1],
                                    OP.mult, OP.add)
                P.inc(i, "v")
                i = e.tensor_scalar(st_mm[:, 1:2], st_mm[:, 1:2],
                                    st_r3[:, T:T + 1], st_nm3[:, T:T + 1],
                                    OP.mult, OP.add)
                P.inc(i, "v")
                for _ in range(8):
                    i = e.engine_nop()
                    P.inc(i, "v")
                i = e.tensor_scalar(st_mm[:, 1:2], st_mm[:, 1:2], -1.0, None,
                                    OP.mult)
                P.inc(i, "v")
                for _ in range(8):
                    i = e.engine_nop()
                    P.inc(i, "v")
                i = e.tensor_tensor(st_mm[:, 0:1], st_mm[:, 0:1],
                                    st_mm[:, 1:2], OP.max)
                P.inc(i, "v")
                for _ in range(8):
                    i = e.engine_nop()
                    P.inc(i, "v")
                i = e.tensor_tensor(st_run[:, 0:1], st_run[:, 0:1],
                                    st_mm[:, 0:1], OP.max)
                P.inc(i, "v", label=f"arun_{w}_{tau}")
            P.inc(e.engine_nop(), "v", label=f"xfree_{w}")
        P.inc(e.engine_nop(), "v", label="amax_ready")
        if not PHASEB:
            return
        P.wait(e, "Tam")
        i = e.reduce_max(st_sc[0:1, 3:4], psum[0:1, PS_SM:PS_SM + 128], AX.X)
        P.inc(i, "v", label="amaxc")
        P.wait(e, "amax_g")
        i = e.tensor_scalar(st_sc[0:1, 4:5], st_sc[0:1, 4:5], 1e-12, None,
                            OP.max)
        P.inc(i, "v", label="amax_d")
        P.wait(e, "sc_ln1")
        i = e.tensor_scalar(st_e[0:1, 3:4], st_e[0:1, 3:4], -1.0, LN448,
                            OP.mult, OP.add)
        P.inc(i, "v", label="sc_l2")
        P.wait(e, "sc_exp1")
        i = e.tensor_scalar(st_sc[0:1, 5:6], st_sc[0:1, 5:6], 448.0, None,
                            OP.min)
        P.inc(i, "v", label="sc_sx")
        P.wait(e, "sc_xs")
        i = e.tensor_tensor(st_sc[0:1, 7:8], st_sc[0:1, 6:7],
                            st_sc[0:1, 2:3], OP.mult)
        P.inc(i, "v", label="sc_sws")
        P.wait(e, "sc_swsr")
        i = e.tensor_scalar(bias_pre[0:1, :], bias_sb[0:1, :],
                            st_sc[0:1, 2:3], None, OP.mult)
        P.inc(i, "v", label="bias_pre")
        i = e.tensor_scalar(st_e[0:1, 1:2], st_sc[0:1, 7:8], C16, None,
                            OP.mult)
        P.inc(i, "v")
        i = e.engine_nop()
        P.inc(i, "v")
        i = e.tensor_copy(st_e[0:1, 0:1], st_sc[0:1, 5:6])
        P.inc(i, "v", label="bcast_src")
        P.wait(e, "Tbc")
        i = e.tensor_copy(st_bc[:, :], psum[:, PS_SM + 128:PS_SM + 130])
        P.inc(i, "v", label="bc")
        i = e.engine_nop()
        P.inc(i, "v")
        i = e.engine_nop()
        P.inc(i, "v")
        i = e.tensor_scalar(st_rsx[:, :], st_r3[:, :], st_bc[:, 0:1], None,
                            OP.mult)
        P.inc(i, "v")
        i = e.tensor_scalar(st_nmx[:, :], st_nm3[:, :], st_bc[:, 0:1], None,
                            OP.mult)
        P.inc(i, "v", label="rsx")
        for T in range(ntile):
            i = e.tensor_scalar(t0f[:, :], x1[:, T * D:(T + 1) * D],
                                st_rsx[:, T:T + 1], st_nmx[:, T:T + 1],
                                OP.mult, OP.add)
            P.inc(i, "v", label=f"ffn_t0_{T}")
            P.wait(e, f"ffn_z_{T}")
            i = e.scalar_tensor_tensor(t0f[:, :], zf[:, :], C20I, zf[:, :],
                                       OP.mult, OP.add)
            P.inc(i, "v", label=f"ffn_u_{T}")
            P.wait(e, f"ffn_mm_{T}")
            ab = PS_ACC + (T % 2) * 1024
            i = e.tensor_scalar(zf[:, :], psum[:, ab:ab + 1024],
                                st_bc[:, 1:2], None, OP.mult)
            P.inc(i, "v", label=f"ffn_z2_{T}")
            i = e.scalar_tensor_tensor(t0f[:, :], zf[:, :], C16I, zf[:, :],
                                       OP.mult, OP.add)
            P.inc(i, "v")
            i = e.tensor_tensor(psum[:, PS_F:PS_F + 1024], t0f[:, :],
                                zf[:, :], OP.subtract)
            P.inc(i, "v", label=f"ffn_fout_{T}")
            i = e.bn_stats(st_bn[:, 0:6], psum[:, PS_F:PS_F + 512])
            P.inc(i, "v")
            i = e.bn_stats(st_bn[:, 6:12], psum[:, PS_F + 512:PS_F + 1024])
            P.inc(i, "v")
            for _ in range(4):
                i = e.engine_nop()
                P.inc(i, "v")
            i = e.bn_aggr(st_mv[:, 0:2].rearrange("p (a b) -> p a b", a=1),
                          st_bn[:, 0:12].rearrange("p (g s) -> p g s", s=6))
            P.inc(i, "v", label=f"bn4_eps_{T}")
            P.wait(e, f"r4_{T}")
            i = e.tensor_tensor(st_r[:, 1:2], st_mv[:, 0:1], st_r[:, 0:1],
                                OP.mult)
            P.inc(i, "v")
            for _ in range(8):
                i = e.engine_nop()
                P.inc(i, "v")
            i = e.tensor_scalar(st_r[:, 1:2], st_r[:, 1:2], -1.0, None,
                                OP.mult)
            P.inc(i, "v", label=f"nm4_{T}")
            P.wait(e, f"o1_{T}")
            if T >= 1:
                P.wait(e, f"out_{T - 1}")
            i = e.tensor_tensor(outb[:, :], zf[:, :], x1[:, T * D:(T + 1) * D],
                                OP.add)
            P.inc(i, "v", label=f"outb_{T}")

    def prog_tensor(e):
        P.wait(e, "setup_done")
        P.wait(e, "wfa_7")
        i = e.matmul(psum[0:1, PS_SM:PS_SM + 128], st_mm[:, 0:1],
                     identf[:, :], start=True, stop=True)
        P.inc(i, "pe", label="Twam")
        P.wait(e, "scale_w")
        i = e.matmul(psum[0:128, PS_SM + 128:PS_SM + 129], onesf_row[0:1, :],
                     st_sc[0:1, 1:2], start=True, stop=True)
        P.inc(i, "pe", label="Twbc")
        P.wait(e, "setup_done")
        for w in range(nwin):
            for c in range(8):
                base = PS_T0 if c % 2 == 0 else PS_T1
                if c >= 2:
                    P.wait(e, f"xnTe_{w}_{c - 2}")
                elif w > 0:
                    P.wait(e, f"xnTe_{w - 1}_{c + 6}")
                    P.wait(e, f"kTe_{w - 1}_1_{c + 6}")
                    if c == 0:
                        P.wait(e, f"ynorm_{w - 1}_1")
                for tau in range(2):
                    P.wait(e, f"xn_{w}_{tau}")
                    i = e.matmul(
                        psum[:, base + tau * 128:base + tau * 128 + 128],
                        xn[:, tau * 1024 + c * 128:tau * 1024 + (c + 1) * 128],
                        identb[:, :], start=True, stop=True)
                P.inc(i, "pe", label=f"TxnT_{w}_{c}")
            for tau in (range(2) if STAGE >= 2 else []):
                P.wait(e, f"xnTe_{w}_7")
                for p in ("q", "k", "v"):
                    P.wait(e, "setup_done")
                if w > 0 and tau == 0:
                    P.wait(e, f"ropeq_{w - 1}_1")
                    P.wait(e, f"ropek_{w - 1}_1")
                    P.wait(e, f"sqk_{w - 1}_1")
                    P.wait(e, f"vev_{w - 1}_1")
                    P.wait(e, f"x1_{w - 1}_1")
                if tau == 1:
                    P.wait(e, f"ropeq_{w}_0")
                    P.wait(e, f"ropek_{w}_0")
                    P.wait(e, f"sqk_{w}_0")
                    P.wait(e, f"vev_{w}_0")
                for c in range(8):
                    lhs = xnT[:, c * 256 + tau * 128: c * 256 + tau * 128 + 128]
                    for pi, p in enumerate(("q", "k", "v")):
                        pb = (PS_Q, PS_K, PS_V)[pi]
                        for o in range(2):
                            i = e.matmul(
                                psum[:, pb + o * 512:pb + (o + 1) * 512], lhs,
                                wT[p][:, c * 1024 + o * 512:
                                      c * 1024 + (o + 1) * 512],
                                start=(c == 0), stop=(c == 7))
                P.inc(i, "pe", label=f"qkv_{w}_{tau}")
            for tau in (range(2) if STAGE >= 4 else []):
                P.wait(e, f"ropeq_{w}_{tau}")
                P.wait(e, f"ropek_{w}_{tau}")
                for c in range(8):
                    base = PS_T0 if c % 2 == 0 else PS_T1
                    if c >= 2:
                        P.wait(e, f"kTe_{w}_{tau}_{c - 2}")
                    elif tau == 1:
                        P.wait(e, f"kTe_{w}_0_{c + 6}")
                    elif w > 0:
                        P.wait(e, f"kTe_{w - 1}_1_{c + 6}")
                    i = e.matmul(psum[:, base:base + 128],
                                 qrot[:, c * 128:(c + 1) * 128], identb[:, :],
                                 start=True, stop=True)
                    P.inc(i, "pe", label=f"TqT_{w}_{tau}_{c}")
                    i = e.matmul(psum[:, base + 128:base + 256],
                                 krot[:, c * 128:(c + 1) * 128], identb[:, :],
                                 start=True, stop=True)
                    P.inc(i, "pe", label=f"TkT_{w}_{tau}_{c}")
            for gi in (range(4) if STAGE >= 5 else []):
                for cc2 in (2 * gi, 2 * gi + 1):
                    P.wait(e, f"qTe_{w}_1_{cc2}")
                    P.wait(e, f"kTe_{w}_1_{cc2}")
                if gi > 0:
                    P.wait(e, f"exp_{w}_{gi - 1}")
                elif w > 0:
                    P.wait(e, f"x1_{w - 1}_1")
                for nh, hi in enumerate((0, 2, 1, 3)):
                    h = 4 * gi + hi
                    c = h // 2
                    r0 = 64 * (h % 2)
                    if nh == 2:
                        i = e.matmul(psum[:, 1024 + 128 * hi:1024 + 128 * hi + 1],
                                     identb[:, :], ones_col[:, :],
                                     start=True, stop=True)
                        P.inc(i, "pe")
                    i = e.matmul(
                        psum[:, PS_SC + 256 * hi:PS_SC + 256 * hi + 256],
                        kT[r0:r0 + 64, c * 256:c * 256 + 128],
                        qT[r0:r0 + 64, c * 256:c * 256 + 256],
                        start=True, stop=True)
                    P.inc(i, "pe")
                    i = e.matmul(
                        psum[:, 1024 + 128 * hi:1024 + 128 * hi + 128],
                        kT[r0:r0 + 64, c * 256 + 128:c * 256 + 256],
                        qT[r0:r0 + 64, c * 256 + 128:c * 256 + 256],
                        start=True, stop=True)
                    P.inc(i, "pe", label=(f"sc_{w}_{gi}" if nh == 3 else None))
                P.wait(e, f"mask_{w}_{gi}")
                if gi == 0:
                    P.wait(e, f"vev_{w}_1")
                    if w > 0:
                        P.wait(e, f"ynorm_{w - 1}_1")
                        P.wait(e, f"RS_{w - 1}_1")
                for hi in (range(4) if STAGE >= 6 else []):
                    h = 4 * gi + hi
                    p00 = probsT[:, 256 * hi:256 * hi + 128]
                    p10 = probsT[:, 256 * hi + 128:256 * hi + 256]
                    p11 = probsT[:, 1024 + 128 * hi:1024 + 128 * hi + 128]
                    v0 = v_sb[:, h * 64:h * 64 + 64]
                    v1 = v_sb[:, 1024 + h * 64:1024 + h * 64 + 64]
                    i = e.matmul(psum[:, PS_Y0 + h * 64:PS_Y0 + h * 64 + 64],
                                 p00, v0, start=True, stop=True)
                    P.inc(i, "pe")
                    i = e.matmul(psum[:, PS_S0 + h:PS_S0 + h + 1], p00,
                                 ones_col[:, :], start=True, stop=True)
                    P.inc(i, "pe")
                    i = e.matmul(psum[:, PS_Y1 + h * 64:PS_Y1 + h * 64 + 64],
                                 p10, v0, start=True, stop=False)
                    P.inc(i, "pe")
                    i = e.matmul(psum[:, PS_S1 + h:PS_S1 + h + 1], p10,
                                 ones_col[:, :], start=True, stop=False)
                    P.inc(i, "pe")
                    i = e.matmul(psum[:, PS_Y1 + h * 64:PS_Y1 + h * 64 + 64],
                                 p11, v1, start=False, stop=True)
                    P.inc(i, "pe")
                    i = e.matmul(psum[:, PS_S1 + h:PS_S1 + h + 1], p11,
                                 ones_col[:, :], start=False, stop=True)
                    P.inc(i, "pe",
                          label=(f"pv_{w}" if (gi == 3 and hi == 3) else None))
            for tau in (range(2) if STAGE >= 7 else []):
                P.wait(e, f"ynorm_{w}_{tau}")
                if tau == 0:
                    P.wait(e, f"RS_{w}_1")
                for c in range(8):
                    base = PS_TY0 if c % 2 == 0 else PS_TY1
                    if c >= 2:
                        P.wait(e, f"yTe_{w}_{tau}_{c - 2}")
                    elif tau == 1:
                        P.wait(e, f"yTe_{w}_0_{c + 6}")
                    elif w > 0:
                        P.wait(e, f"yTe_{w - 1}_1_{c + 6}")
                    i = e.matmul(psum[:, base:base + 128],
                                 y_sb[:, c * 128:(c + 1) * 128], identb[:, :],
                                 start=True, stop=True)
                    P.inc(i, "pe", label=f"TyT_{w}_{tau}_{c}")
                P.wait(e, f"yTe_{w}_{tau}_7")
                P.wait(e, "setup_done")
                if tau == 0:
                    P.wait(e, f"exp_{w}_3")
                ob = PS_O + tau * 1024
                for c in range(8):
                    lhs = yT[:, c * 128:(c + 1) * 128]
                    for o in range(2):
                        i = e.matmul(psum[:, ob + o * 512:ob + (o + 1) * 512],
                                     lhs,
                                     wT["o"][:, c * 1024 + o * 512:
                                             c * 1024 + (o + 1) * 512],
                                     start=(c == 0), stop=(c == 7))
                P.inc(i, "pe", label=f"oproj_{w}_{tau}")
        if not PHASEB:
            return
        P.wait(e, "amax_ready")
        i = e.matmul(psum[0:1, PS_SM:PS_SM + 128], st_run[:, 0:1],
                     identf[:, :], start=True, stop=True)
        P.inc(i, "pe", label="Tam")
        P.wait(e, "bcast_src")
        i = e.matmul(psum[0:128, PS_SM + 128:PS_SM + 130], onesf_row[0:1, :],
                     st_e[0:1, 0:2], start=True, stop=True)
        P.inc(i, "pe", label="Tbc")
        for T in range(ntile):
            P.wait(e, f"ffn_q8_{T}")
            for c in range(8):
                base = PS_T0 if c % 2 == 0 else PS_T1
                if c >= 2:
                    P.wait(e, f"q8Te_{T}_{c - 2}")
                elif T > 0:
                    P.wait(e, f"q8Te_{T - 1}_{c + 6}")
                i = e.matmul(psum[:, base:base + 128],
                             sq[:, c * 128:(c + 1) * 128], identb[:, :],
                             start=True, stop=True)
                P.inc(i, "pe", label=f"Tq8_{T}_{c}")
            P.wait(e, f"q8Te_{T}_7")
            P.wait(e, "bias_pre")
            if T >= 2:
                P.wait(e, f"ffn_z2_{T - 2}")
            ab = PS_ACC + (T % 2) * 1024
            for c in range(8):
                for o in range(2):
                    i = e.matmul(psum[:, ab + o * 512:ab + (o + 1) * 512],
                                 xmb[:, c * 128:(c + 1) * 128],
                                 wf8[:, c * 1024 + o * 512:
                                     c * 1024 + (o + 1) * 512],
                                 start=(c == 0), stop=False)
                    P.inc(i, "pe")
            for o in range(2):
                i = e.matmul(psum[:, ab + o * 512:ab + (o + 1) * 512],
                             ones_row[0:1, :],
                             bias_pre[0:1, o * 512:(o + 1) * 512],
                             start=False, stop=True)
                P.inc(i, "pe", label=(f"ffn_mm_{T}" if o == 1 else None))

    progs = [("sync", prog_sync), ("gpsimd", prog_gpsimd),
             ("scalar", prog_scalar), ("vector", prog_vector),
             ("tensor", prog_tensor)]
    P.strict = bool(STAGE >= 7 and PHASEB)
    P.reset("count")
    fake = _FakeEng()
    for nm, f in progs:
        P.cur = nm
        P.trace[nm] = []
        f(fake)
    P.check_deadlock()
    P.reset("emit")
    with nc.Block() as block:
        block.sync(prog_sync)
        block.gpsimd(prog_gpsimd)
        block.scalar(prog_scalar)
        block.vector(prog_vector)
        block.tensor(prog_tensor)
    es.close()
    return nc


def _host_tables(nwin=NWIN):
    ntile = 2 * nwin
    inv = (1.0 / 10000.0 ** (np.arange(0, HD, 2, dtype=np.float32) /
                             np.float32(HD))).astype(np.float32)
    pos = np.arange(2 * TC, dtype=np.float32)
    fr = np.outer(pos, inv).astype(np.float32)
    cos = np.cos(fr).astype(np.float32)
    sin = np.sin(fr).astype(np.float32)

    def core_layout(tab, h):
        t = tab[h * TC:h * TC + ntile * 128]
        return np.ascontiguousarray(
            t.reshape(ntile, 128, 32).transpose(1, 0, 2).reshape(128, ntile * 32))
    mask = (np.arange(128)[None, :] >= np.arange(128)[:, None]).astype(np.float32)
    identf = np.eye(128, dtype=np.float32)
    return cos, sin, core_layout, mask, identf


def kernel(L0_attn_x, L0_attn_wq, L0_attn_wk, L0_attn_wv, L0_attn_wo,
           L0_attn_q_gain, L0_ffn_weight, L0_ffn_bias):
    x = np.ascontiguousarray(np.asarray(L0_attn_x, dtype=np.float32))
    wqt = np.ascontiguousarray(np.asarray(L0_attn_wq, np.float32).T)
    wkt = np.ascontiguousarray(np.asarray(L0_attn_wk, np.float32).T)
    wvt = np.ascontiguousarray(np.asarray(L0_attn_wv, np.float32).T)
    wot = np.ascontiguousarray(np.asarray(L0_attn_wo, np.float32).T)
    wft = np.ascontiguousarray(np.asarray(L0_ffn_weight, np.float32).T)
    bias = np.ascontiguousarray(np.asarray(L0_ffn_bias, np.float32)[None, :])
    g = np.asarray(L0_attn_q_gain, np.float32)
    grep = np.ascontiguousarray(np.broadcast_to(g[None, :], (128, H)))

    cos, sin, core_layout, mask, identf = _host_tables()
    nc = build_program()
    in_maps = []
    for c in range(8):
        b, hh = c // 2, c % 2
        in_maps.append({
            "x": np.ascontiguousarray(x[b, hh * TC:(hh + 1) * TC]),
            "wq": wqt, "wk": wkt, "wv": wvt, "wo": wot, "wf": wft,
            "bias": bias, "grep": grep,
            "cosr": core_layout(cos, hh), "sinr": core_layout(sin, hh),
            "mask": mask, "identf": identf,
        })
    res = run_bass_kernel_spmd(nc, in_maps, core_ids=list(range(8)))
    outs = [np.asarray(res.results[c]["out"], np.float32) for c in range(8)]
    return np.stack([np.concatenate([outs[2 * b], outs[2 * b + 1]], axis=0)
                     for b in range(4)], axis=0)


# version 03 (final)
# speedup vs baseline: 1.0196x; 1.0196x over previous
import contextlib
import numpy as np
import ml_dtypes

import concourse.bass as bass
import concourse.mybir as mybir
from concourse.bass_utils import run_bass_kernel_spmd
from concourse.alu_op_type import AluOpType

F32 = mybir.dt.float32
BF16 = mybir.dt.bfloat16
AF = mybir.ActivationFunctionType
AX = mybir.AxisListType
OP = AluOpType

D = 1024
H = 16
HD = 64
TC = 4096
NWIN = 16
CC_ENABLE = True
PHASEB = True
STAGE = 8
SAFE5 = False
DEBUG_DUMP = False
LN_EPS = 1e-5
RMS_EPS = 1.1920928955078125e-07
LN448 = float(np.log(np.float32(448.0)))
C20 = float(2.0 ** 20)
C20I = float(2.0 ** -20)
C16 = float(2.0 ** 16)
C16I = float(2.0 ** -16)

PS_Q, PS_K, PS_V = 0, 1024, 2048
PS_T0, PS_T1 = 3072, 3584
PS_TY0, PS_TY1 = 1024, 1568
PS_SM = 3840
PS_SC = 0
PS_S0, PS_S1 = 1536, 1552
PS_Y0, PS_Y1 = 2048, 3072
PS_O = 0
PS_ACC = 0
PS_F = 2048


class _FakeInstr:
    def then_inc(self, *a, **k):
        return self


class _FakeEng:
    def __getattr__(self, name):
        def f(*a, **k):
            return _FakeInstr()
        return f


class Plan:
    def __init__(self):
        self.mode = "count"
        self.counts = {}
        self.events = {}
        self.handles = {}
        self.trace = {}
        self.cur = None

    def reset(self, mode):
        self.mode = mode
        self.counts = {}

    def inc(self, instr, sem, k=1, label=None):
        c = self.counts.get(sem, 0) + k
        self.counts[sem] = c
        if self.mode == "count":
            if label is not None:
                assert label not in self.events, f"dup label {label}"
                self.events[label] = (sem, c)
            self.trace[self.cur].append(("i", sem, k, label))
        else:
            instr.then_inc(self.handles[sem], k)
        return c

    def wait(self, eng, label):
        if self.mode == "count":
            self.trace[self.cur].append(("w", label, None, None))
        elif self.mode == "emit":
            if label not in self.events:
                assert not getattr(self, "strict", True), f"missing {label}"
                return
            sem, val = self.events[label]
            eng.wait_ge(self.handles[sem], val)

    def wait_val(self, eng, sem, val):
        if self.mode == "count":
            self.trace[self.cur].append(("wv", sem, val, None))
        elif self.mode == "emit":
            eng.wait_ge(self.handles[sem], val)

    def check_deadlock(self):
        pcs = {k: 0 for k in self.trace}
        counts = {}
        progressed = True
        while progressed:
            progressed = False
            for eng, items in self.trace.items():
                while pcs[eng] < len(items):
                    kind, a, b, lbl = items[pcs[eng]]
                    if kind == "i":
                        counts[a] = counts.get(a, 0) + b
                    elif kind == "w" and a not in self.events:
                        assert not getattr(self, "strict", True), f"missing {a}"
                        pcs[eng] += 1
                        progressed = True
                        continue
                    else:
                        sem, val = (self.events[a] if kind == "w"
                                    else (a, b))
                        if counts.get(sem, 0) < val:
                            break
                    pcs[eng] += 1
                    progressed = True
        stuck = {e: pcs[e] for e in pcs if pcs[e] < len(self.trace[e])}
        if stuck:
            lines = []
            for eng, pc in stuck.items():
                kind, a, b, lbl = self.trace[eng][pc]
                sem, val = (self.events.get(a, ("?", -1)) if kind == "w"
                            else (a, b))
                lines.append(f"{eng}@{pc}: waits {a if kind == 'w' else sem}"
                             f" (sem {sem} >= {val}, have {counts.get(sem, 0)})")
            raise RuntimeError("plan deadlock:\n" + "\n".join(lines))


def build_program(nwin=NWIN):
    ntile = 2 * nwin
    nc = bass.Bass("TRN2", target_bir_lowering=False,
                   detect_race_conditions=False)

    x_in = nc.dram_tensor("x", [nwin * 256, D], F32, kind="ExternalInput").ap()
    wts = {p: nc.dram_tensor(f"w{p}", [D, D], F32, kind="ExternalInput").ap()
           for p in ("q", "k", "v", "o", "f")}
    bias_in = nc.dram_tensor("bias", [1, D], F32, kind="ExternalInput").ap()
    g_in = nc.dram_tensor("grep", [128, H], F32, kind="ExternalInput").ap()
    cos_in = nc.dram_tensor("cosr", [128, ntile * 32], F32, kind="ExternalInput").ap()
    sin_in = nc.dram_tensor("sinr", [128, ntile * 32], F32, kind="ExternalInput").ap()
    mask_in = nc.dram_tensor("mask", [128, 128], F32, kind="ExternalInput").ap()
    identf_in = nc.dram_tensor("identf", [128, 128], F32, kind="ExternalInput").ap()
    out_hbm = nc.dram_tensor("out", [nwin * 256, D], F32, kind="ExternalOutput").ap()
    dbg = (nc.dram_tensor("dbg", [128, 12288], F32, kind="ExternalOutput").ap()
           if DEBUG_DUMP else None)
    cc_i = nc.dram_tensor("cc_i", [1, 1], F32)
    cc_o = nc.dram_tensor("cc_o", [1, 1], F32)

    P = Plan()
    es = contextlib.ExitStack()
    sb = lambda name, shape, dt: es.enter_context(nc.sbuf_tensor(name, shape, dt))

    wT = {p: sb(f"wT{p}", [128, 8 * D], BF16) for p in ("q", "k", "v", "o")}
    wf8 = sb("wf8", [128, 8 * D], BF16)
    x1 = sb("x1", [128, ntile * D], BF16)
    x_sb = sb("x_sb", [128, 2 * 2048], BF16)
    xn = sb("xn", [128, 2048], BF16)
    xnT = sb("xnT", [128, 2048], BF16)
    qrot = sb("qrot", [128, D], BF16)
    krot = sb("krot", [128, D], BF16)
    qT = sb("qT", [128, 2048], BF16)
    kT = sb("kT", [128, 2048], BF16)
    v_sb = sb("v_sb", [128, 2048], BF16)
    probsT = xnT
    y_sb = qrot
    yT = krot
    sq = sb("sq", [128, D], BF16)
    xmb = sb("xmb", [128, D], BF16)
    cos_sb = sb("cos_sb", [128, ntile * 32], BF16)
    sin_sb = sb("sin_sb", [128, ntile * 32], BF16)
    cq_t = sb("cq_t", [128, 512], BF16)
    sq_t = sb("sq_t", [128, 512], BF16)
    mask_sb = sb("mask_sb", [128, 128], BF16)
    identb = sb("identb", [128, 128], BF16)
    identf = sb("identf_sb", [128, 128], F32)
    ones_col = sb("ones_col", [128, 1], BF16)
    ones_row = sb("ones_row", [1, 128], BF16)
    onesf_row = sb("onesf_row", [1, 128], F32)
    g8 = sb("g8", [128, H], F32)
    bias_sb = sb("bias_sb", [1, D], F32)
    bias_pre = sb("bias_pre", [1, D], BF16)
    t0f = sb("t0f", [128, D], F32)
    zf = sb("zf", [128, D], F32)
    outb = sb("outb", [128, D], F32)
    st_bn = sb("st_bn", [128, 24], F32)
    st_mv = sb("st_mv", [128, 8], F32)
    st_r = sb("st_r", [128, 4], F32)
    st_m = sb("st_m", [128, 64], F32)
    st_R = sb("st_R", [128, 64], F32)
    st_s = sb("st_s", [128, 32], F32)
    st_r3 = sb("st_r3", [128, ntile], F32)
    st_nm3 = sb("st_nm3", [128, ntile], F32)
    st_rsx = sb("st_rsx", [128, ntile], F32)
    st_nmx = sb("st_nmx", [128, ntile], F32)
    st_run = sb("st_run", [128, 1], F32)
    st_mm = sb("st_mm", [128, 2], F32)
    st_sc = sb("st_sc", [1, 8], F32)
    st_bc = sb("st_bc", [128, 2], F32)
    st_e = sb("st_e", [128, 4], F32)
    st_snap = sb("st_snap", [128, 28], F32) if DEBUG_DUMP else None
    st_eps = sb("st_eps", [128, 2], F32)
    st_nop = sb("st_nop", [128, 1], F32)
    psum = es.enter_context(nc.psum_tensor("ps", [128, 4096], F32))

    sems = {}
    for name in ("pe", "v", "s", "g", "dg", "dx0", "dx1", "dwt", "dwf0",
                 "dwf1", "dout", "cc"):
        sems[name] = es.enter_context(nc.semaphore(name))
    P.handles = sems

    def heads3(ap):
        return ap.rearrange("p (h j) -> p h j", h=H)

    def half3(ap, lo):
        return ap.rearrange("p (h j) -> p h j", h=H)[:, :, lo:lo + 32]

    def bcast_h(ap32):
        return ap32.unsqueeze(1).broadcast_to([128, H, 32])

    def prog_sync(e):
        if PHASEB:
            for T in range(ntile):
                P.wait(e, f"outb_{T}")
                i = e.dma_start(out=out_hbm[T * 128:(T + 1) * 128, :],
                                in_=outb[:, :])
                P.inc(i, "dout", 16, label=f"out_{T}")
            dbg_extra = (160 if PHASEB else 80) if DEBUG_DUMP else 0
            P.wait_val(e, "dout", 16 * ntile + dbg_extra)

    def prog_gpsimd(e):
        for p in ("q", "k", "v", "o"):
            i = e.dma_start(
                out=wT[p][:, :].rearrange("p (c j) -> p c j", c=8),
                in_=wts[p].rearrange("(c p) j -> p c j", c=8))
            P.inc(i, "dwt", 16)
        for dst, srcap in ((cos_sb, cos_in), (sin_sb, sin_in),
                           (mask_sb, mask_in), (identb, identf_in),
                           (identf, identf_in), (g8, g_in),
                           (bias_sb, bias_in)):
            i = e.dma_start(out=dst[:, :], in_=srcap)
            P.inc(i, "dwt", 16,
                  label=("setup_done" if dst is bias_sb else None))
        def load_x(w):
            if w >= 2:
                P.wait(e, f"xfree_{w - 2}")
            for tau in range(2):
                i = e.dma_start(
                    out=x_sb[:, (w % 2) * 2048 + tau * 1024:
                             (w % 2) * 2048 + (tau + 1) * 1024],
                    in_=x_in[w * 256 + tau * 128: w * 256 + tau * 128 + 128, :])
                P.inc(i, "dx0" if w % 2 == 0 else "dx1", 16,
                      label=(f"x_{w}" if tau == 1 else None))
        load_x(0)
        if nwin > 1:
            load_x(1)
        for c in (range(8) if PHASEB else []):
            buf = t0f if c % 2 == 0 else zf
            if c >= 2:
                P.wait(e, f"wfa_{c - 2}")
            i = e.dma_start(out=buf[:, :], in_=wts["f"][c * 128:(c + 1) * 128, :])
            P.inc(i, "dwf0" if c % 2 == 0 else "dwf1", 16, label=f"wf1_{c}")
            P.wait(e, f"wfr_{c}")
            i = e.engine_nop()
            P.inc(i, "g", label=f"wfs_{c}")
        for c in (range(8) if PHASEB else []):
            buf = t0f if c % 2 == 0 else zf
            P.wait(e, "wfa_7" if c < 2 else f"wfq_sub_{c - 2}")
            i = e.dma_start(out=buf[:, :], in_=wts["f"][c * 128:(c + 1) * 128, :])
            P.inc(i, "dwf0" if c % 2 == 0 else "dwf1", 16, label=f"wf2_{c}")
        for w in range(nwin):
            if STAGE >= 3:
                P.wait(e, f"vev_{w}_1")
            for tau in (range(2) if STAGE >= 7 else []):
                P.wait(e, f"a1_{w}_{tau}")
                i = e.engine_nop()
                P.inc(i, "g", label=f"a2_{w}_{tau}")
            if w + 2 < nwin:
                load_x(w + 2)
        if DEBUG_DUMP:
            P.wait(e, f"x1_{nwin - 1}_1")
            for name, t, width in (("qT", qT, 2048), ("kT", kT, 2048),
                                   ("pT", xnT, 2048), ("y", y_sb, 1024),
                                   ("v", v_sb, 2048)):
                pass
            offs = 0
            for t, width in ((xn, 2048), (v_sb, 2048), (qT, 2048),
                             (kT, 2048), (x1, 2048)):
                i = e.dma_start(out=dbg[:, offs:offs + width], in_=t[:, 0:width])
                P.inc(i, "dout", 16)
                offs += width
        if not PHASEB:
            if STAGE >= 7:
                for T in range(ntile):
                    P.wait(e, f"x1_{T // 2}_{T % 2}")
                    i = e.dma_start(out=out_hbm[T * 128:(T + 1) * 128, :],
                                    in_=x1[:, T * D:(T + 1) * D])
                    P.inc(i, "dout", 16)
            else:
                P.wait(e, f"xfree_{nwin - 1}")
                for T in range(ntile):
                    i = e.dma_start(out=out_hbm[T * 128:(T + 1) * 128, :],
                                    in_=x_sb[:, (T % 4) * 1024:
                                             (T % 4) * 1024 + 1024])
                    P.inc(i, "dout", 16)
            P.wait_val(e, "dout", 16 * ntile)
            return
        P.wait(e, "amaxc")
        i = e.dma_start(out=cc_i.ap(), in_=st_sc[0:1, 3:4])
        P.inc(i, "dg", 16)
        P.wait_val(e, "dg", P.counts["dg"])
        if CC_ENABLE:
            i = e.collective_compute(
                "AllReduce", OP.max, replica_groups=[list(range(8))],
                ins=[cc_i.ap().opt()], outs=[cc_o.ap().opt()])
            P.inc(i, "cc", 1)
            P.wait_val(e, "cc", 1)
        else:
            i = e.dma_start(out=cc_o.ap(), in_=cc_i.ap())
            P.inc(i, "cc", 16)
            P.wait_val(e, "cc", 16)
        i = e.dma_start(out=st_sc[0:1, 4:5], in_=cc_o.ap())
        P.inc(i, "dg", 16, label="amax_g")
        for T in range(ntile):
            P.wait(e, f"ffn_t0_{T}")
            i = e.tensor_scalar(t0f[:, :], t0f[:, :], 448.0, -448.0,
                                OP.min, OP.max)
            P.inc(i, "g", label=f"ffn_tc_{T}")
            if T > 0:
                P.wait(e, f"Tq8_{T - 1}_7")
            P.wait(e, f"ffn_u_{T}")
            i = e.tensor_tensor(sq[:, :], zf[:, :], t0f[:, :], OP.subtract)
            P.inc(i, "g", label=f"ffn_q8_{T}")
        if DEBUG_DUMP:
            P.wait(e, f"outb_{ntile - 1}")
            i = e.dma_start(out=dbg[0:1, 12000:12008], in_=st_sc[0:1, 0:8])
            P.inc(i, "dout", 16)
            offs2 = 10240
            for t, width in ((st_r3, ntile), (st_nm3, ntile),
                             (st_run, 1), (st_mm, 2)):
                i = e.dma_start(out=dbg[:, offs2:offs2 + width],
                                in_=t[:, 0:width],
                                allow_slow_non_contiguous=True)
                P.inc(i, "dout", 16)
                offs2 += width

    def prog_scalar(e):
        if PHASEB:
            P.wait(e, "wam_d")
            i = e.activation(st_e[0:1, 3:4], st_sc[0:1, 0:1], AF.Ln)
            P.inc(i, "s", label="wln_d")
            e.copy(st_nop[:, 0:1], st_nop[:, 0:1])
            e.copy(st_nop[:, 0:1], st_nop[:, 0:1])
            P.wait(e, "wl2")
            i = e.activation(st_sc[0:1, 1:2], st_e[0:1, 3:4], AF.Exp)
            P.inc(i, "s", label="wsc_exp")
            P.wait(e, "scale_w")
            i = e.activation(st_e[0:1, 3:4], st_sc[0:1, 1:2], AF.Ln)
            P.inc(i, "s")
            e.copy(st_nop[:, 0:1], st_nop[:, 0:1])
            e.copy(st_nop[:, 0:1], st_nop[:, 0:1])
            i = e.activation(st_sc[0:1, 2:3], st_e[0:1, 3:4], AF.Exp,
                             scale=-1.0)
            P.inc(i, "s", label="wsr")
            for c in range(8):
                P.wait(e, f"wfq_tc_{c}")
                if c > 0:
                    P.wait(e, f"wfq_u_{c - 1}")
                buf = t0f if c % 2 == 0 else zf
                i = e.activation(outb[:, :], buf[:, :], AF.Identity, scale=C20)
                P.inc(i, "s", label=f"wfq_z_{c}")
        for w in range(nwin):
            sl = (w % 2) * 2048
            for tau in range(2):
                P.wait(e, f"bnx_eps_{w}_{tau}")
                i = e.activation(st_e[:, 0:1],
                                 st_mv[:, tau * 2 + 1:tau * 2 + 2], AF.Ln,
                                 bias=st_eps[:, 0:1])
                P.inc(i, "s")
                e.copy(st_nop[:, 0:1], st_nop[:, 0:1])
                e.copy(st_nop[:, 0:1], st_nop[:, 0:1])
                i = e.activation(st_r[:, tau * 2:tau * 2 + 1], st_e[:, 0:1],
                                 AF.Exp, scale=-0.5)
                P.inc(i, "s", label=f"r1_{w}_{tau}")
                P.wait(e, f"nmr_{w}_{tau}")
                if w > 0 and tau == 0:
                    P.wait(e, f"TxnT_{w - 1}_7")
                i = e.activation(xn[:, tau * 1024:(tau + 1) * 1024],
                                 x_sb[:, sl + tau * 1024: sl + (tau + 1) * 1024],
                                 AF.Identity,
                                 scale=st_r[:, tau * 2:tau * 2 + 1],
                                 bias=st_r[:, tau * 2 + 1:tau * 2 + 2])
                P.inc(i, "s", label=f"xn_{w}_{tau}")
            for c in range(8):
                P.wait(e, f"TxnT_{w}_{c}")
                base = PS_T0 if c % 2 == 0 else PS_T1
                i = e.copy(xnT[:, c * 256:(c + 1) * 256],
                           psum[:, base:base + 256])
                P.inc(i, "s", label=f"xnTe_{w}_{c}")
            for tau in (range(2) if STAGE >= 3 else []):
                P.wait(e, f"qkv_{w}_{tau}")
                if not (w == 0 and tau == 0):
                    pw, pt = (w, 0) if tau == 1 else (w - 1, 1)
                    P.wait(e, f"mk_{pw}_{pt}")
                i = e.activation(sq[:, :], psum[:, PS_Q:PS_Q + 1024], AF.Square)
                P.inc(i, "s", label=f"sqq_{w}_{tau}")
                P.wait(e, f"mq_{w}_{tau}")
                i = e.activation(sq[:, :], psum[:, PS_K:PS_K + 1024], AF.Square)
                P.inc(i, "s", label=f"sqk_{w}_{tau}")
                if w > 0 and tau == 0:
                    P.wait(e, f"pv_{w - 1}")
                i = e.copy(v_sb[:, tau * 1024:(tau + 1) * 1024],
                           psum[:, PS_V:PS_V + 1024])
                P.inc(i, "s", label=f"vev_{w}_{tau}")
                P.wait(e, f"mk_{w}_{tau}")
                i = e.activation(st_R[:, tau * 32:tau * 32 + 16],
                                 st_m[:, tau * 32:tau * 32 + 16], AF.Ln,
                                 scale=1.0 / 64.0, bias=st_eps[:, 1:2])
                P.inc(i, "s")
                e.copy(st_nop[:, 0:1], st_nop[:, 0:1])
                e.copy(st_nop[:, 0:1], st_nop[:, 0:1])
                i = e.activation(st_R[:, tau * 32:tau * 32 + 16],
                                 st_R[:, tau * 32:tau * 32 + 16], AF.Exp,
                                 scale=-0.5)
                P.inc(i, "s", label=f"Rq_{w}_{tau}")
                i = e.activation(st_R[:, tau * 32 + 16:tau * 32 + 32],
                                 st_m[:, tau * 32 + 16:tau * 32 + 32], AF.Ln,
                                 scale=1.0 / 64.0, bias=st_eps[:, 1:2])
                P.inc(i, "s")
                e.copy(st_nop[:, 0:1], st_nop[:, 0:1])
                e.copy(st_nop[:, 0:1], st_nop[:, 0:1])
                i = e.activation(st_R[:, tau * 32 + 16:tau * 32 + 32],
                                 st_R[:, tau * 32 + 16:tau * 32 + 32], AF.Exp,
                                 scale=-0.5)
                P.inc(i, "s", label=f"Rk_{w}_{tau}")
            for tau in (range(2) if STAGE >= 4 else []):
                for c in range(8):
                    P.wait(e, f"TqT_{w}_{tau}_{c}")
                    base = PS_T0 if c % 2 == 0 else PS_T1
                    i = e.copy(qT[:, c * 256 + tau * 128:
                                  c * 256 + tau * 128 + 128],
                               psum[:, base:base + 128])
                    P.inc(i, "s", label=f"qTe_{w}_{tau}_{c}")
                    P.wait(e, f"TkT_{w}_{tau}_{c}")
                    i = e.copy(kT[:, c * 256 + tau * 128:
                                  c * 256 + tau * 128 + 128],
                               psum[:, base + 128:base + 256])
                    P.inc(i, "s", label=f"kTe_{w}_{tau}_{c}")
            for gi in (range(4) if STAGE >= 5.5 else []):
                P.wait(e, f"sc_{w}_{gi}")
                i = e.activation(probsT[:, 0:1024],
                                 psum[:, PS_SC:PS_SC + 1024], AF.Exp)
                P.inc(i, "s")
                i = e.activation(probsT[:, 1024:1536],
                                 psum[:, 1024:1536], AF.Exp)
                P.inc(i, "s", label=f"exp_{w}_{gi}")
            for tau in (range(2) if STAGE >= 6 else []):
                P.wait(e, f"pv_{w}")
                base = PS_S0 if tau == 0 else PS_S1
                i = e.activation(st_s[:, tau * 16:tau * 16 + 16],
                                 psum[:, base:base + 16], AF.Ln)
                P.inc(i, "s")
                e.copy(st_nop[:, 0:1], st_nop[:, 0:1])
                e.copy(st_nop[:, 0:1], st_nop[:, 0:1])
                i = e.activation(st_s[:, tau * 16:tau * 16 + 16],
                                 st_s[:, tau * 16:tau * 16 + 16], AF.Exp,
                                 scale=-1.0)
                P.inc(i, "s", label=f"RS_{w}_{tau}")
            for tau in (range(2) if STAGE >= 7 else []):
                for c in range(8):
                    P.wait(e, f"TyT_{w}_{tau}_{c}")
                    base = PS_TY0 if c % 2 == 0 else PS_TY1
                    i = e.copy(yT[:, c * 128:(c + 1) * 128],
                               psum[:, base:base + 128])
                    P.inc(i, "s", label=f"yTe_{w}_{tau}_{c}")
                P.wait(e, f"bno_eps_{w}_{tau}")
                i = e.activation(st_e[:, 1:2],
                                 st_mv[:, 4 + tau * 2 + 1:4 + tau * 2 + 2],
                                 AF.Ln, bias=st_eps[:, 0:1])
                P.inc(i, "s")
                e.copy(st_nop[:, 0:1], st_nop[:, 0:1])
                e.copy(st_nop[:, 0:1], st_nop[:, 0:1])
                i = e.activation(st_r[:, tau * 2:tau * 2 + 1], st_e[:, 1:2],
                                 AF.Exp, scale=-0.5)
                P.inc(i, "s", label=f"r2_{w}_{tau}")
                P.wait(e, f"nm2_{w}_{tau}")
                i = e.activation(xmb[:, :],
                                 x_sb[:, (w % 2) * 2048 + tau * 1024:
                                      (w % 2) * 2048 + (tau + 1) * 1024],
                                 AF.Identity,
                                 bias=st_r[:, tau * 2 + 1:tau * 2 + 2])
                P.inc(i, "s", label=f"xmb_{w}_{tau}")
                T = 2 * w + tau
                P.wait(e, f"bn3_eps_{w}_{tau}")
                i = e.activation(st_e[:, 2:3],
                                 st_mv[:, 4 + tau * 2 + 1:4 + tau * 2 + 2],
                                 AF.Ln, bias=st_eps[:, 0:1])
                P.inc(i, "s")
                e.copy(st_nop[:, 0:1], st_nop[:, 0:1])
                e.copy(st_nop[:, 0:1], st_nop[:, 0:1])
                i = e.activation(st_r3[:, T:T + 1], st_e[:, 2:3], AF.Exp,
                                 scale=-0.5)
                P.inc(i, "s", label=f"r3_{w}_{tau}")
        if not PHASEB:
            return
        P.wait(e, "amax_d")
        i = e.activation(st_e[0:1, 3:4], st_sc[0:1, 4:5], AF.Ln)
        P.inc(i, "s", label="sc_ln1")
        e.copy(st_nop[:, 0:1], st_nop[:, 0:1])
        e.copy(st_nop[:, 0:1], st_nop[:, 0:1])
        P.wait(e, "sc_l2")
        i = e.activation(st_sc[0:1, 5:6], st_e[0:1, 3:4], AF.Exp)
        P.inc(i, "s", label="sc_exp1")
        P.wait(e, "sc_sx")
        i = e.activation(st_e[0:1, 3:4], st_sc[0:1, 5:6], AF.Ln)
        P.inc(i, "s")
        e.copy(st_nop[:, 0:1], st_nop[:, 0:1])
        e.copy(st_nop[:, 0:1], st_nop[:, 0:1])
        i = e.activation(st_sc[0:1, 6:7], st_e[0:1, 3:4], AF.Exp, scale=-1.0)
        P.inc(i, "s", label="sc_xs")
        P.wait(e, "sc_sws")
        i = e.activation(st_e[0:1, 3:4], st_sc[0:1, 7:8], AF.Ln)
        P.inc(i, "s")
        e.copy(st_nop[:, 0:1], st_nop[:, 0:1])
        e.copy(st_nop[:, 0:1], st_nop[:, 0:1])
        i = e.activation(st_sc[0:1, 2:3], st_e[0:1, 3:4], AF.Exp, scale=-1.0)
        P.inc(i, "s", label="sc_swsr")
        for T in range(ntile):
            P.wait(e, f"ffn_tc_{T}")
            if T > 0:
                P.wait(e, f"outb_{T - 1}")
            i = e.activation(zf[:, :], t0f[:, :], AF.Identity, scale=C20)
            P.inc(i, "s", label=f"ffn_z_{T}")
            for c in range(8):
                P.wait(e, f"Tq8_{T}_{c}")
                base = PS_T0 if c % 2 == 0 else PS_T1
                i = e.copy(xmb[:, c * 128:(c + 1) * 128],
                           psum[:, base:base + 128])
                P.inc(i, "s", label=f"q8Te_{T}_{c}")
            P.wait(e, f"bn4_eps_{T}")
            i = e.activation(st_e[:, 2:3],
                             st_mv[:, 1:2], AF.Ln, bias=st_eps[:, 0:1])
            P.inc(i, "s")
            e.copy(st_nop[:, 0:1], st_nop[:, 0:1])
            e.copy(st_nop[:, 0:1], st_nop[:, 0:1])
            i = e.activation(st_r[:, 0:1], st_e[:, 2:3], AF.Exp, scale=-0.5)
            P.inc(i, "s", label=f"r4_{T}")
            P.wait(e, f"nm4_{T}")
            i = e.activation(zf[:, :], psum[:, PS_F:PS_F + 1024], AF.Identity,
                             scale=st_r[:, 0:1], bias=st_r[:, 1:2])
            P.inc(i, "s", label=f"o1_{T}")

    def prog_vector(e):
        i = e.memset(st_nop[:, 0:1], 0.0)
        P.inc(i, "v")
        i = e.memset(st_eps[:, 0:1], LN_EPS)
        P.inc(i, "v")
        i = e.memset(st_eps[:, 1:2], RMS_EPS)
        P.inc(i, "v")
        i = e.memset(ones_col[:, :], 1.0)
        P.inc(i, "v")
        i = e.memset(ones_row[:, :], 1.0)
        P.inc(i, "v")
        i = e.memset(onesf_row[:, :], 1.0)
        P.inc(i, "v")
        i = e.memset(st_run[:, :], 0.0)
        P.inc(i, "v")
        i = e.memset(st_mm[:, 0:1], 0.0)
        P.inc(i, "v")
        P.wait(e, "setup_done")
        i = e.tensor_scalar(g8[:, :], g8[:, :], 0.125, None, OP.mult)
        P.inc(i, "v", label="g8")
        for c in range(8):
            buf = t0f if c % 2 == 0 else zf
            P.wait(e, f"wf1_{c}")
            i = e.tensor_reduce(st_mm[:, 1:2], buf[:, :], AX.X, OP.max)
            P.inc(i, "v")
            i = e.tensor_reduce(st_e[:, 3:4], buf[:, :], AX.X, OP.min)
            P.inc(i, "v", label=f"wfr_{c}")
            P.wait(e, f"wfs_{c}")
            i = e.tensor_scalar(st_e[:, 3:4], st_e[:, 3:4], -1.0, None,
                                OP.mult)
            P.inc(i, "v")
            for _ in range(8):
                i = e.engine_nop()
                P.inc(i, "v")
            i = e.tensor_tensor(st_mm[:, 1:2], st_mm[:, 1:2], st_e[:, 3:4],
                                OP.max)
            P.inc(i, "v")
            for _ in range(8):
                i = e.engine_nop()
                P.inc(i, "v")
            i = e.tensor_tensor(st_mm[:, 0:1], st_mm[:, 0:1], st_mm[:, 1:2],
                                OP.max)
            P.inc(i, "v", label=f"wfa_{c}")
        P.wait(e, "Twam")
        i = e.reduce_max(st_sc[0:1, 0:1], psum[0:1, PS_SM:PS_SM + 128], AX.X)
        P.inc(i, "v")
        for _ in range(4):
            i = e.engine_nop()
            P.inc(i, "v")
        i = e.tensor_scalar(st_sc[0:1, 0:1], st_sc[0:1, 0:1], 1e-12, None,
                            OP.max)
        P.inc(i, "v", label="wam_d")
        P.wait(e, "wln_d")
        i = e.tensor_scalar(st_e[0:1, 3:4], st_e[0:1, 3:4], -1.0, LN448,
                            OP.mult, OP.add)
        P.inc(i, "v", label="wl2")
        P.wait(e, "wsc_exp")
        i = e.tensor_scalar(st_sc[0:1, 1:2], st_sc[0:1, 1:2], 448.0, None,
                            OP.min)
        P.inc(i, "v", label="scale_w")
        P.wait(e, "Twbc")
        i = e.tensor_copy(st_bc[:, 0:1], psum[:, PS_SM + 128:PS_SM + 129])
        P.inc(i, "v", label="wbc")
        for c in range(8):
            buf = t0f if c % 2 == 0 else zf
            P.wait(e, f"wf2_{c}")
            i = e.tensor_scalar(buf[:, :], buf[:, :], st_bc[:, 0:1], 448.0,
                                OP.mult, OP.min)
            P.inc(i, "v")
            i = e.tensor_scalar(buf[:, :], buf[:, :], -448.0, None, OP.max)
            P.inc(i, "v", label=f"wfq_tc_{c}")
            P.wait(e, f"wfq_z_{c}")
            i = e.scalar_tensor_tensor(buf[:, :], outb[:, :], C20I, outb[:, :],
                                       OP.mult, OP.add)
            P.inc(i, "v", label=f"wfq_u_{c}")
            i = e.tensor_tensor(
                wf8[:, :].rearrange("p (c j) -> p c j", c=8)[:, c, :],
                outb[:, :], buf[:, :], OP.subtract)
            P.inc(i, "v", label=f"wfq_sub_{c}")
        for w in range(nwin):
            sl = (w % 2) * 2048
            for tau in range(2):
                P.wait(e, f"x_{w}")
                xt = x_sb[:, sl + tau * 1024: sl + (tau + 1) * 1024]
                i = e.bn_stats(st_bn[:, tau * 12:tau * 12 + 6], xt[:, 0:512])
                P.inc(i, "v")
                i = e.bn_stats(st_bn[:, tau * 12 + 6:tau * 12 + 12],
                               xt[:, 512:1024])
                P.inc(i, "v")
                i = e.engine_nop()
                P.inc(i, "v")
                i = e.engine_nop()
                P.inc(i, "v")
                i = e.engine_nop()
                P.inc(i, "v")
                i = e.engine_nop()
                P.inc(i, "v")
                i = e.bn_aggr(st_mv[:, tau * 2:tau * 2 + 2].rearrange("p (a b) -> p a b", a=1),
                          st_bn[:, tau * 12:tau * 12 + 12].rearrange("p (g s) -> p g s", s=6))
                P.inc(i, "v", label=f"bnx_eps_{w}_{tau}")
                if DEBUG_DUMP and w == 0:
                    i = e.engine_nop()
                    P.inc(i, "v")
                    i = e.engine_nop()
                    P.inc(i, "v")
                    i = e.tensor_scalar(st_snap[:, tau * 14:tau * 14 + 12],
                                        st_bn[:, tau * 12:tau * 12 + 12],
                                        0.0, None, OP.add)
                    P.inc(i, "v")
                    i = e.tensor_scalar(st_snap[:, tau * 14 + 12:tau * 14 + 14],
                                        st_mv[:, tau * 2:tau * 2 + 2],
                                        0.0, None, OP.add)
                    P.inc(i, "v")
                P.wait(e, f"r1_{w}_{tau}")
                i = e.tensor_tensor(st_r[:, tau * 2 + 1:tau * 2 + 2],
                                    st_mv[:, tau * 2:tau * 2 + 1],
                                    st_r[:, tau * 2:tau * 2 + 1], OP.mult)
                P.inc(i, "v")
                i = e.engine_nop()
                P.inc(i, "v")
                i = e.engine_nop()
                P.inc(i, "v")
                i = e.tensor_scalar(st_r[:, tau * 2 + 1:tau * 2 + 2],
                                    st_r[:, tau * 2 + 1:tau * 2 + 2],
                                    -1.0, None, OP.mult)
                P.inc(i, "v", label=f"nmr_{w}_{tau}")
            for tau in (range(2) if STAGE >= 3 else []):
                P.wait(e, f"sqq_{w}_{tau}")
                i = e.tensor_reduce(st_m[:, tau * 32:tau * 32 + 16],
                                    heads3(sq[:, :]), AX.X, OP.add)
                P.inc(i, "v", label=f"mq_{w}_{tau}")
                P.wait(e, f"sqk_{w}_{tau}")
                i = e.tensor_reduce(st_m[:, tau * 32 + 16:tau * 32 + 32],
                                    heads3(sq[:, :]), AX.X, OP.add)
                P.inc(i, "v", label=f"mk_{w}_{tau}")
                P.wait(e, f"Rq_{w}_{tau}")
                i = e.tensor_tensor(st_R[:, tau * 32:tau * 32 + 16],
                                    st_R[:, tau * 32:tau * 32 + 16],
                                    g8[:, :], OP.mult)
                P.inc(i, "v", label=f"Cq_{w}_{tau}")
                i = e.engine_nop()
                P.inc(i, "v")
                i = e.engine_nop()
                P.inc(i, "v")
                Tg = 2 * w + tau
                ct = cos_sb[:, 32 * Tg:32 * Tg + 32]
                st = sin_sb[:, 32 * Tg:32 * Tg + 32]
                Cq3 = st_R[:, tau * 32:tau * 32 + 16].unsqueeze(2)
                i = e.tensor_tensor(
                    cq_t[:, :].rearrange("p (h j) -> p h j", h=H),
                    bcast_h(ct), Cq3.broadcast_to([128, H, 32]), OP.mult)
                P.inc(i, "v")
                i = e.tensor_tensor(
                    sq_t[:, :].rearrange("p (h j) -> p h j", h=H),
                    bcast_h(st), Cq3.broadcast_to([128, H, 32]), OP.mult)
                P.inc(i, "v", label=f"cqt_{w}_{tau}")
                P.wait(e, f"Rk_{w}_{tau}")
                Ck3 = st_R[:, tau * 32 + 16:tau * 32 + 32].unsqueeze(2)
                i = e.tensor_tensor(
                    t0f[:, 512:1024].rearrange("p (h j) -> p h j", h=H),
                    bcast_h(ct), Ck3.broadcast_to([128, H, 32]), OP.mult)
                P.inc(i, "v")
                i = e.tensor_tensor(
                    zf[:, 512:1024].rearrange("p (h j) -> p h j", h=H),
                    bcast_h(st), Ck3.broadcast_to([128, H, 32]), OP.mult)
                P.inc(i, "v", label=f"ckt_{w}_{tau}")
                if tau == 1:
                    P.wait(e, f"TqT_{w}_0_7")
                    P.wait(e, f"TkT_{w}_0_7")
                elif w > 0:
                    P.wait(e, f"TqT_{w - 1}_1_7")
                    P.wait(e, f"TkT_{w - 1}_1_7")
                    P.wait(e, f"TyT_{w - 1}_1_7")
                    P.wait(e, f"oproj_{w - 1}_1")
                q_ps = psum[:, PS_Q:PS_Q + 1024]
                k_ps = psum[:, PS_K:PS_K + 1024]
                cq3 = cq_t[:, :].rearrange("p (h j) -> p h j", h=H)
                sq3 = sq_t[:, :].rearrange("p (h j) -> p h j", h=H)
                ck3 = t0f[:, 512:1024].rearrange("p (h j) -> p h j", h=H)
                sk3 = zf[:, 512:1024].rearrange("p (h j) -> p h j", h=H)
                tA = t0f[:, 0:512].rearrange("p (h j) -> p h j", h=H)
                tB = zf[:, 0:512].rearrange("p (h j) -> p h j", h=H)
                for (dst, src, c3, s3) in ((qrot, q_ps, cq3, sq3),
                                           (krot, k_ps, ck3, sk3)):
                    i = e.tensor_tensor(tA, half3(src, 0), c3, OP.mult)
                    P.inc(i, "v")
                    i = e.tensor_tensor(tB, half3(src, 32), s3, OP.mult)
                    P.inc(i, "v")
                    i = e.tensor_tensor(half3(dst[:, :], 0), tA, tB, OP.add)
                    P.inc(i, "v")
                    i = e.tensor_tensor(tA, half3(src, 32), c3, OP.mult)
                    P.inc(i, "v")
                    i = e.tensor_tensor(tB, half3(src, 0), s3, OP.mult)
                    P.inc(i, "v")
                    i = e.tensor_tensor(half3(dst[:, :], 32), tA, tB,
                                        OP.subtract)
                    P.inc(i, "v",
                          label=("ropeq" if dst is qrot else "ropek")
                          + f"_{w}_{tau}")
            for gi in (range(4) if STAGE >= 5.7 else []):
                P.wait(e, f"exp_{w}_{gi}")
                m3 = mask_sb[:, :].unsqueeze(1).broadcast_to([128, 4, 128])
                p6 = probsT[:, 0:1536].rearrange("p (i q) -> p i q", i=6)
                i = e.tensor_tensor(p6[:, 0:4, 0:128], p6[:, 0:4, 0:128], m3,
                                    OP.mult)
                P.inc(i, "v")
                p4 = probsT[:, 1024:1536].rearrange("p (i q) -> p i q", i=4)
                i = e.tensor_tensor(p4, p4, m3, OP.mult)
                P.inc(i, "v", label=f"mask_{w}_{gi}")
            for tau in (range(2) if STAGE >= 6 else []):
                P.wait(e, f"RS_{w}_{tau}")
                if tau == 1:
                    P.wait(e, f"TyT_{w}_0_7")
                elif w > 0:
                    P.wait(e, f"TyT_{w - 1}_1_7")
                ybase = PS_Y0 if tau == 0 else PS_Y1
                rs3 = st_s[:, tau * 16:tau * 16 + 16].unsqueeze(2)
                i = e.tensor_tensor(heads3(y_sb[:, :]),
                                    heads3(psum[:, ybase:ybase + 1024]),
                                    rs3.broadcast_to([128, H, HD]), OP.mult)
                P.inc(i, "v", label=f"ynorm_{w}_{tau}")
            for tau in (range(2) if STAGE >= 7 else []):
                P.wait(e, f"oproj_{w}_{tau}")
                ob = PS_O + tau * 1024
                i = e.bn_stats(st_bn[:, tau * 12:tau * 12 + 6],
                               psum[:, ob:ob + 512])
                P.inc(i, "v")
                i = e.bn_stats(st_bn[:, tau * 12 + 6:tau * 12 + 12],
                               psum[:, ob + 512:ob + 1024])
                P.inc(i, "v")
                i = e.engine_nop()
                P.inc(i, "v")
                i = e.engine_nop()
                P.inc(i, "v")
                i = e.engine_nop()
                P.inc(i, "v")
                i = e.engine_nop()
                P.inc(i, "v")
                i = e.bn_aggr(st_mv[:, 4 + tau * 2:4 + tau * 2 + 2].rearrange("p (a b) -> p a b", a=1),
                          st_bn[:, tau * 12:tau * 12 + 12].rearrange("p (g s) -> p g s", s=6))
                P.inc(i, "v", label=f"bno_eps_{w}_{tau}")
                P.wait(e, f"r2_{w}_{tau}")
                i = e.tensor_tensor(st_r[:, tau * 2 + 1:tau * 2 + 2],
                                    st_mv[:, 4 + tau * 2:4 + tau * 2 + 1],
                                    st_r[:, tau * 2:tau * 2 + 1], OP.mult)
                P.inc(i, "v")
                i = e.engine_nop()
                P.inc(i, "v")
                i = e.engine_nop()
                P.inc(i, "v")
                i = e.tensor_scalar(st_r[:, tau * 2 + 1:tau * 2 + 2],
                                    st_r[:, tau * 2 + 1:tau * 2 + 2], -1.0,
                                    None, OP.mult)
                P.inc(i, "v", label=f"nm2_{w}_{tau}")
                T = 2 * w + tau
                P.wait(e, f"xmb_{w}_{tau}")
                i = e.scalar_tensor_tensor(x1[:, T * D:(T + 1) * D],
                                           psum[:, ob:ob + 1024],
                                           st_r[:, tau * 2:tau * 2 + 1],
                                           xmb[:, :], OP.mult, OP.add)
                P.inc(i, "v", label=f"x1_{w}_{tau}")
                i = e.bn_stats(st_bn[:, tau * 12:tau * 12 + 6],
                               x1[:, T * D:T * D + 512])
                P.inc(i, "v")
                i = e.bn_stats(st_bn[:, tau * 12 + 6:tau * 12 + 12],
                               x1[:, T * D + 512:T * D + 1024])
                P.inc(i, "v")
                i = e.engine_nop()
                P.inc(i, "v")
                i = e.engine_nop()
                P.inc(i, "v")
                i = e.engine_nop()
                P.inc(i, "v")
                i = e.engine_nop()
                P.inc(i, "v")
                i = e.bn_aggr(st_mv[:, 4 + tau * 2:4 + tau * 2 + 2].rearrange("p (a b) -> p a b", a=1),
                          st_bn[:, tau * 12:tau * 12 + 12].rearrange("p (g s) -> p g s", s=6))
                P.inc(i, "v", label=f"bn3_eps_{w}_{tau}")
                P.wait(e, f"r3_{w}_{tau}")
                i = e.tensor_tensor(st_nm3[:, T:T + 1],
                                    st_mv[:, 4 + tau * 2:4 + tau * 2 + 1],
                                    st_r3[:, T:T + 1], OP.mult)
                P.inc(i, "v")
                i = e.engine_nop()
                P.inc(i, "v")
                i = e.engine_nop()
                P.inc(i, "v")
                i = e.tensor_scalar(st_nm3[:, T:T + 1], st_nm3[:, T:T + 1],
                                    -1.0, None, OP.mult)
                P.inc(i, "v", label=f"nm3_{w}_{tau}")
                i = e.tensor_reduce(st_mm[:, 0:1], x1[:, T * D:(T + 1) * D],
                                    AX.X, OP.max)
                P.inc(i, "v")
                i = e.tensor_reduce(st_mm[:, 1:2], x1[:, T * D:(T + 1) * D],
                                    AX.X, OP.min)
                P.inc(i, "v", label=f"a1_{w}_{tau}")
                P.wait(e, f"a2_{w}_{tau}")
                i = e.tensor_scalar(st_mm[:, 0:1], st_mm[:, 0:1],
                                    st_r3[:, T:T + 1], st_nm3[:, T:T + 1],
                                    OP.mult, OP.add)
                P.inc(i, "v")
                i = e.tensor_scalar(st_mm[:, 1:2], st_mm[:, 1:2],
                                    st_r3[:, T:T + 1], st_nm3[:, T:T + 1],
                                    OP.mult, OP.add)
                P.inc(i, "v")
                for _ in range(8):
                    i = e.engine_nop()
                    P.inc(i, "v")
                i = e.tensor_scalar(st_mm[:, 1:2], st_mm[:, 1:2], -1.0, None,
                                    OP.mult)
                P.inc(i, "v")
                for _ in range(8):
                    i = e.engine_nop()
                    P.inc(i, "v")
                i = e.tensor_tensor(st_mm[:, 0:1], st_mm[:, 0:1],
                                    st_mm[:, 1:2], OP.max)
                P.inc(i, "v")
                for _ in range(8):
                    i = e.engine_nop()
                    P.inc(i, "v")
                i = e.tensor_tensor(st_run[:, 0:1], st_run[:, 0:1],
                                    st_mm[:, 0:1], OP.max)
                P.inc(i, "v", label=f"arun_{w}_{tau}")
            P.inc(e.engine_nop(), "v", label=f"xfree_{w}")
        P.inc(e.engine_nop(), "v", label="amax_ready")
        if not PHASEB:
            return
        P.wait(e, "Tam")
        i = e.reduce_max(st_sc[0:1, 3:4], psum[0:1, PS_SM:PS_SM + 128], AX.X)
        P.inc(i, "v", label="amaxc")
        P.wait(e, "amax_g")
        i = e.tensor_scalar(st_sc[0:1, 4:5], st_sc[0:1, 4:5], 1e-12, None,
                            OP.max)
        P.inc(i, "v", label="amax_d")
        P.wait(e, "sc_ln1")
        i = e.tensor_scalar(st_e[0:1, 3:4], st_e[0:1, 3:4], -1.0, LN448,
                            OP.mult, OP.add)
        P.inc(i, "v", label="sc_l2")
        P.wait(e, "sc_exp1")
        i = e.tensor_scalar(st_sc[0:1, 5:6], st_sc[0:1, 5:6], 448.0, None,
                            OP.min)
        P.inc(i, "v", label="sc_sx")
        P.wait(e, "sc_xs")
        i = e.tensor_tensor(st_sc[0:1, 7:8], st_sc[0:1, 6:7],
                            st_sc[0:1, 2:3], OP.mult)
        P.inc(i, "v", label="sc_sws")
        P.wait(e, "sc_swsr")
        i = e.tensor_scalar(bias_pre[0:1, :], bias_sb[0:1, :],
                            st_sc[0:1, 2:3], None, OP.mult)
        P.inc(i, "v", label="bias_pre")
        i = e.tensor_scalar(st_e[0:1, 1:2], st_sc[0:1, 7:8], C16, None,
                            OP.mult)
        P.inc(i, "v")
        i = e.engine_nop()
        P.inc(i, "v")
        i = e.tensor_copy(st_e[0:1, 0:1], st_sc[0:1, 5:6])
        P.inc(i, "v", label="bcast_src")
        P.wait(e, "Tbc")
        i = e.tensor_copy(st_bc[:, :], psum[:, PS_SM + 128:PS_SM + 130])
        P.inc(i, "v", label="bc")
        i = e.engine_nop()
        P.inc(i, "v")
        i = e.engine_nop()
        P.inc(i, "v")
        i = e.tensor_scalar(st_rsx[:, :], st_r3[:, :], st_bc[:, 0:1], None,
                            OP.mult)
        P.inc(i, "v")
        i = e.tensor_scalar(st_nmx[:, :], st_nm3[:, :], st_bc[:, 0:1], None,
                            OP.mult)
        P.inc(i, "v", label="rsx")
        for T in range(ntile):
            i = e.tensor_scalar(t0f[:, :], x1[:, T * D:(T + 1) * D],
                                st_rsx[:, T:T + 1], st_nmx[:, T:T + 1],
                                OP.mult, OP.add)
            P.inc(i, "v", label=f"ffn_t0_{T}")
            P.wait(e, f"ffn_z_{T}")
            i = e.scalar_tensor_tensor(t0f[:, :], zf[:, :], C20I, zf[:, :],
                                       OP.mult, OP.add)
            P.inc(i, "v", label=f"ffn_u_{T}")
            P.wait(e, f"ffn_mm_{T}")
            ab = PS_ACC + (T % 2) * 1024
            i = e.tensor_scalar(zf[:, :], psum[:, ab:ab + 1024],
                                st_bc[:, 1:2], None, OP.mult)
            P.inc(i, "v", label=f"ffn_z2_{T}")
            i = e.scalar_tensor_tensor(t0f[:, :], zf[:, :], C16I, zf[:, :],
                                       OP.mult, OP.add)
            P.inc(i, "v")
            i = e.tensor_tensor(psum[:, PS_F:PS_F + 1024], t0f[:, :],
                                zf[:, :], OP.subtract)
            P.inc(i, "v", label=f"ffn_fout_{T}")
            i = e.bn_stats(st_bn[:, 0:6], psum[:, PS_F:PS_F + 512])
            P.inc(i, "v")
            i = e.bn_stats(st_bn[:, 6:12], psum[:, PS_F + 512:PS_F + 1024])
            P.inc(i, "v")
            for _ in range(4):
                i = e.engine_nop()
                P.inc(i, "v")
            i = e.bn_aggr(st_mv[:, 0:2].rearrange("p (a b) -> p a b", a=1),
                          st_bn[:, 0:12].rearrange("p (g s) -> p g s", s=6))
            P.inc(i, "v", label=f"bn4_eps_{T}")
            P.wait(e, f"r4_{T}")
            i = e.tensor_tensor(st_r[:, 1:2], st_mv[:, 0:1], st_r[:, 0:1],
                                OP.mult)
            P.inc(i, "v")
            for _ in range(8):
                i = e.engine_nop()
                P.inc(i, "v")
            i = e.tensor_scalar(st_r[:, 1:2], st_r[:, 1:2], -1.0, None,
                                OP.mult)
            P.inc(i, "v", label=f"nm4_{T}")
            P.wait(e, f"o1_{T}")
            if T >= 1:
                P.wait(e, f"out_{T - 1}")
            i = e.tensor_tensor(outb[:, :], zf[:, :], x1[:, T * D:(T + 1) * D],
                                OP.add)
            P.inc(i, "v", label=f"outb_{T}")

    def prog_tensor(e):
        P.wait(e, "setup_done")
        P.wait(e, "wfa_7")
        i = e.matmul(psum[0:1, PS_SM:PS_SM + 128], st_mm[:, 0:1],
                     identf[:, :], start=True, stop=True)
        P.inc(i, "pe", label="Twam")
        P.wait(e, "scale_w")
        i = e.matmul(psum[0:128, PS_SM + 128:PS_SM + 129], onesf_row[0:1, :],
                     st_sc[0:1, 1:2], start=True, stop=True)
        P.inc(i, "pe", label="Twbc")
        P.wait(e, "setup_done")
        for w in range(nwin):
            for c in range(8):
                base = PS_T0 if c % 2 == 0 else PS_T1
                if c >= 2:
                    P.wait(e, f"xnTe_{w}_{c - 2}")
                elif w > 0:
                    P.wait(e, f"xnTe_{w - 1}_{c + 6}")
                    P.wait(e, f"kTe_{w - 1}_1_{c + 6}")
                    if c == 0:
                        P.wait(e, f"ynorm_{w - 1}_1")
                for tau in range(2):
                    P.wait(e, f"xn_{w}_{tau}")
                    i = e.matmul(
                        psum[:, base + tau * 128:base + tau * 128 + 128],
                        xn[:, tau * 1024 + c * 128:tau * 1024 + (c + 1) * 128],
                        identb[:, :], start=True, stop=True)
                P.inc(i, "pe", label=f"TxnT_{w}_{c}")
            for tau in (range(2) if STAGE >= 2 else []):
                P.wait(e, f"xnTe_{w}_7")
                for p in ("q", "k", "v"):
                    P.wait(e, "setup_done")
                if w > 0 and tau == 0:
                    P.wait(e, f"ropeq_{w - 1}_1")
                    P.wait(e, f"ropek_{w - 1}_1")
                    P.wait(e, f"sqk_{w - 1}_1")
                    P.wait(e, f"vev_{w - 1}_1")
                    P.wait(e, f"x1_{w - 1}_1")
                if tau == 1:
                    P.wait(e, f"ropeq_{w}_0")
                    P.wait(e, f"ropek_{w}_0")
                    P.wait(e, f"sqk_{w}_0")
                    P.wait(e, f"vev_{w}_0")
                for c in range(8):
                    lhs = xnT[:, c * 256 + tau * 128: c * 256 + tau * 128 + 128]
                    for pi, p in enumerate(("q", "k", "v")):
                        pb = (PS_Q, PS_K, PS_V)[pi]
                        for o in range(2):
                            i = e.matmul(
                                psum[:, pb + o * 512:pb + (o + 1) * 512], lhs,
                                wT[p][:, c * 1024 + o * 512:
                                      c * 1024 + (o + 1) * 512],
                                start=(c == 0), stop=(c == 7))
                P.inc(i, "pe", label=f"qkv_{w}_{tau}")
            for tau in (range(2) if STAGE >= 4 else []):
                P.wait(e, f"ropeq_{w}_{tau}")
                P.wait(e, f"ropek_{w}_{tau}")
                for c in range(8):
                    base = PS_T0 if c % 2 == 0 else PS_T1
                    if c >= 2:
                        P.wait(e, f"kTe_{w}_{tau}_{c - 2}")
                    elif tau == 1:
                        P.wait(e, f"kTe_{w}_0_{c + 6}")
                    elif w > 0:
                        P.wait(e, f"kTe_{w - 1}_1_{c + 6}")
                    i = e.matmul(psum[:, base:base + 128],
                                 qrot[:, c * 128:(c + 1) * 128], identb[:, :],
                                 start=True, stop=True)
                    P.inc(i, "pe", label=f"TqT_{w}_{tau}_{c}")
                    i = e.matmul(psum[:, base + 128:base + 256],
                                 krot[:, c * 128:(c + 1) * 128], identb[:, :],
                                 start=True, stop=True)
                    P.inc(i, "pe", label=f"TkT_{w}_{tau}_{c}")
            for gi in (range(4) if STAGE >= 5 else []):
                for cc2 in (2 * gi, 2 * gi + 1):
                    P.wait(e, f"qTe_{w}_1_{cc2}")
                    P.wait(e, f"kTe_{w}_1_{cc2}")
                if gi > 0:
                    P.wait(e, f"exp_{w}_{gi - 1}")
                elif w > 0:
                    P.wait(e, f"x1_{w - 1}_1")
                for nh, hi in enumerate((0, 2, 1, 3)):
                    h = 4 * gi + hi
                    c = h // 2
                    r0 = 64 * (h % 2)
                    if nh == 2:
                        i = e.matmul(psum[:, 1024 + 128 * hi:1024 + 128 * hi + 1],
                                     identb[:, :], ones_col[:, :],
                                     start=True, stop=True)
                        P.inc(i, "pe")
                    i = e.matmul(
                        psum[:, PS_SC + 256 * hi:PS_SC + 256 * hi + 256],
                        kT[r0:r0 + 64, c * 256:c * 256 + 128],
                        qT[r0:r0 + 64, c * 256:c * 256 + 256],
                        start=True, stop=True)
                    P.inc(i, "pe")
                    i = e.matmul(
                        psum[:, 1024 + 128 * hi:1024 + 128 * hi + 128],
                        kT[r0:r0 + 64, c * 256 + 128:c * 256 + 256],
                        qT[r0:r0 + 64, c * 256 + 128:c * 256 + 256],
                        start=True, stop=True)
                    P.inc(i, "pe", label=(f"sc_{w}_{gi}" if nh == 3 else None))
                P.wait(e, f"mask_{w}_{gi}")
                if gi == 0:
                    P.wait(e, f"vev_{w}_1")
                    if w > 0:
                        P.wait(e, f"ynorm_{w - 1}_1")
                        P.wait(e, f"RS_{w - 1}_1")
                for hi in (range(4) if STAGE >= 6 else []):
                    h = 4 * gi + hi
                    p00 = probsT[:, 256 * hi:256 * hi + 128]
                    p10 = probsT[:, 256 * hi + 128:256 * hi + 256]
                    p11 = probsT[:, 1024 + 128 * hi:1024 + 128 * hi + 128]
                    v0 = v_sb[:, h * 64:h * 64 + 64]
                    v1 = v_sb[:, 1024 + h * 64:1024 + h * 64 + 64]
                    i = e.matmul(psum[:, PS_Y0 + h * 64:PS_Y0 + h * 64 + 64],
                                 p00, v0, start=True, stop=True)
                    P.inc(i, "pe")
                    i = e.matmul(psum[:, PS_S0 + h:PS_S0 + h + 1], p00,
                                 ones_col[:, :], start=True, stop=True)
                    P.inc(i, "pe")
                    i = e.matmul(psum[:, PS_Y1 + h * 64:PS_Y1 + h * 64 + 64],
                                 p10, v0, start=True, stop=False)
                    P.inc(i, "pe")
                    i = e.matmul(psum[:, PS_S1 + h:PS_S1 + h + 1], p10,
                                 ones_col[:, :], start=True, stop=False)
                    P.inc(i, "pe")
                    i = e.matmul(psum[:, PS_Y1 + h * 64:PS_Y1 + h * 64 + 64],
                                 p11, v1, start=False, stop=True)
                    P.inc(i, "pe")
                    i = e.matmul(psum[:, PS_S1 + h:PS_S1 + h + 1], p11,
                                 ones_col[:, :], start=False, stop=True)
                    P.inc(i, "pe",
                          label=(f"pv_{w}" if (gi == 3 and hi == 3) else None))
            for tau in (range(2) if STAGE >= 7 else []):
                P.wait(e, f"ynorm_{w}_{tau}")
                if tau == 0:
                    P.wait(e, f"RS_{w}_1")
                for c in range(8):
                    base = PS_TY0 if c % 2 == 0 else PS_TY1
                    if c >= 2:
                        P.wait(e, f"yTe_{w}_{tau}_{c - 2}")
                    elif tau == 1:
                        P.wait(e, f"yTe_{w}_0_{c + 6}")
                    elif w > 0:
                        P.wait(e, f"yTe_{w - 1}_1_{c + 6}")
                    i = e.matmul(psum[:, base:base + 128],
                                 y_sb[:, c * 128:(c + 1) * 128], identb[:, :],
                                 start=True, stop=True)
                    P.inc(i, "pe", label=f"TyT_{w}_{tau}_{c}")
                P.wait(e, f"yTe_{w}_{tau}_7")
                P.wait(e, "setup_done")
                if tau == 0:
                    P.wait(e, f"exp_{w}_3")
                ob = PS_O + tau * 1024
                for c in range(8):
                    lhs = yT[:, c * 128:(c + 1) * 128]
                    for o in range(2):
                        i = e.matmul(psum[:, ob + o * 512:ob + (o + 1) * 512],
                                     lhs,
                                     wT["o"][:, c * 1024 + o * 512:
                                             c * 1024 + (o + 1) * 512],
                                     start=(c == 0), stop=(c == 7))
                P.inc(i, "pe", label=f"oproj_{w}_{tau}")
        if not PHASEB:
            return
        P.wait(e, "amax_ready")
        i = e.matmul(psum[0:1, PS_SM:PS_SM + 128], st_run[:, 0:1],
                     identf[:, :], start=True, stop=True)
        P.inc(i, "pe", label="Tam")
        P.wait(e, "bcast_src")
        i = e.matmul(psum[0:128, PS_SM + 128:PS_SM + 130], onesf_row[0:1, :],
                     st_e[0:1, 0:2], start=True, stop=True)
        P.inc(i, "pe", label="Tbc")
        for T in range(ntile):
            P.wait(e, f"ffn_q8_{T}")
            for c in range(8):
                base = PS_T0 if c % 2 == 0 else PS_T1
                if c >= 2:
                    P.wait(e, f"q8Te_{T}_{c - 2}")
                elif T > 0:
                    P.wait(e, f"q8Te_{T - 1}_{c + 6}")
                i = e.matmul(psum[:, base:base + 128],
                             sq[:, c * 128:(c + 1) * 128], identb[:, :],
                             start=True, stop=True)
                P.inc(i, "pe", label=f"Tq8_{T}_{c}")
            P.wait(e, f"q8Te_{T}_7")
            P.wait(e, "bias_pre")
            if T >= 2:
                P.wait(e, f"ffn_z2_{T - 2}")
            ab = PS_ACC + (T % 2) * 1024
            for c in range(8):
                for o in range(2):
                    i = e.matmul(psum[:, ab + o * 512:ab + (o + 1) * 512],
                                 xmb[:, c * 128:(c + 1) * 128],
                                 wf8[:, c * 1024 + o * 512:
                                     c * 1024 + (o + 1) * 512],
                                 start=(c == 0), stop=False)
                    P.inc(i, "pe")
            for o in range(2):
                i = e.matmul(psum[:, ab + o * 512:ab + (o + 1) * 512],
                             ones_row[0:1, :],
                             bias_pre[0:1, o * 512:(o + 1) * 512],
                             start=False, stop=True)
                P.inc(i, "pe", label=(f"ffn_mm_{T}" if o == 1 else None))

    progs = [("sync", prog_sync), ("gpsimd", prog_gpsimd),
             ("scalar", prog_scalar), ("vector", prog_vector),
             ("tensor", prog_tensor)]
    P.strict = bool(STAGE >= 7 and PHASEB)
    P.reset("count")
    fake = _FakeEng()
    for nm, f in progs:
        P.cur = nm
        P.trace[nm] = []
        f(fake)
    P.check_deadlock()
    P.reset("emit")
    with nc.Block() as block:
        block.sync(prog_sync)
        block.gpsimd(prog_gpsimd)
        block.scalar(prog_scalar)
        block.vector(prog_vector)
        block.tensor(prog_tensor)
    es.close()
    return nc


def _host_tables(nwin=NWIN):
    ntile = 2 * nwin
    inv = (1.0 / 10000.0 ** (np.arange(0, HD, 2, dtype=np.float32) /
                             np.float32(HD))).astype(np.float32)
    pos = np.arange(2 * TC, dtype=np.float32)
    fr = np.outer(pos, inv).astype(np.float32)
    cos = np.cos(fr).astype(np.float32)
    sin = np.sin(fr).astype(np.float32)

    def core_layout(tab, h):
        t = tab[h * TC:h * TC + ntile * 128]
        return np.ascontiguousarray(
            t.reshape(ntile, 128, 32).transpose(1, 0, 2).reshape(128, ntile * 32))
    mask = (np.arange(128)[None, :] >= np.arange(128)[:, None]).astype(np.float32)
    identf = np.eye(128, dtype=np.float32)
    return cos, sin, core_layout, mask, identf


def kernel(L0_attn_x, L0_attn_wq, L0_attn_wk, L0_attn_wv, L0_attn_wo,
           L0_attn_q_gain, L0_ffn_weight, L0_ffn_bias):
    x = np.ascontiguousarray(np.asarray(L0_attn_x, dtype=np.float32))
    wqt = np.ascontiguousarray(np.asarray(L0_attn_wq, np.float32).T)
    wkt = np.ascontiguousarray(np.asarray(L0_attn_wk, np.float32).T)
    wvt = np.ascontiguousarray(np.asarray(L0_attn_wv, np.float32).T)
    wot = np.ascontiguousarray(np.asarray(L0_attn_wo, np.float32).T)
    wft = np.ascontiguousarray(np.asarray(L0_ffn_weight, np.float32).T)
    bias = np.ascontiguousarray(np.asarray(L0_ffn_bias, np.float32)[None, :])
    g = np.asarray(L0_attn_q_gain, np.float32)
    grep = np.ascontiguousarray(np.broadcast_to(g[None, :], (128, H)))

    cos, sin, core_layout, mask, identf = _host_tables()
    nc = build_program()
    in_maps = []
    for c in range(8):
        b, hh = c // 2, c % 2
        in_maps.append({
            "x": np.ascontiguousarray(x[b, hh * TC:(hh + 1) * TC]),
            "wq": wqt, "wk": wkt, "wv": wvt, "wo": wot, "wf": wft,
            "bias": bias, "grep": grep,
            "cosr": core_layout(cos, hh), "sinr": core_layout(sin, hh),
            "mask": mask, "identf": identf,
        })
    res = run_bass_kernel_spmd(nc, in_maps, core_ids=list(range(8)))
    outs = [np.asarray(res.results[c]["out"], np.float32) for c in range(8)]
    return np.stack([np.concatenate([outs[2 * b], outs[2 * b + 1]], axis=0)
                     for b in range(4)], axis=0)
